# version 01
# baseline (speedup 1.0000x reference)
import numpy as np
import ml_dtypes
import concourse.bass as bass
import concourse.mybir as mybir
from concourse.bass_utils import run_bass_kernel_spmd
from concourse.bass_isa_rust import ReduceOp

F32 = mybir.dt.float32
BF16 = mybir.dt.bfloat16
AX = mybir.AxisListType
OP = mybir.AluOpType
ACT = mybir.ActivationFunctionType

D = 1024
NCORE = 8
CH = 64
NP = 8
FF = 4096
SEG = 512
SEGB = 256
BASE_LR = 0.001
BASE_WD = 0.9


class Plan:

    NROT = 8
    ENGS = ("t", "v", "s", "g", "y")
    SEMS = tuple(["t", "v", "s", "g"]
                 + [f"dy{i}" for i in range(8)] + [f"dg{i}" for i in range(8)])

    def __init__(self):
        self.ops = {e: [] for e in self.ENGS}
        self.cnt = {sm: 0 for sm in self.SEMS}
        self.dma_i = {"y": 0, "g": 0}
        self._waited = {e: {sm: 0 for sm in self.SEMS} for e in self.ENGS}

    def op(self, eng, fn, waits=(), inc=True, dma=False):
        w = list(waits)
        if dma:
            i = self.dma_i[eng]
            self.dma_i[eng] += 1
            sem_name = f"d{eng}{i % self.NROT}"
            amt = 16
            if self.cnt[sem_name] > 0:
                w.append((sem_name, self.cnt[sem_name]))
        else:
            sem_name = eng
            amt = 1
        tok = None
        if inc:
            self.cnt[sem_name] += amt
            tok = self.cnt[sem_name]
        ww = []
        for (sm, c) in w:
            if c is None or c <= 0:
                continue
            if self._waited[eng][sm] >= c:
                continue
            self._waited[eng][sm] = c
            ww.append((sm, c))
        self.ops[eng].append((fn, tuple(ww), sem_name if inc else None, amt))
        return tok

    def barrier_tokens(self):
        return dict(self.cnt)


def emit_engine(plan, eng, engobj, sems):
    for fn, waits, inc_sem, amt in plan.ops[eng]:
        for (sm, c) in waits:
            engobj.wait_ge(sems[sm], c)
        inst = fn(engobj)
        if inc_sem is not None:
            assert inst is not None, f"op on {eng} must return instruction to inc"
            inst.then_inc(sems[inc_sem], amt)


def host_constants():
    c = {}
    c["c_ident"] = np.eye(128, dtype=ml_dtypes.bfloat16)
    c["c_identf"] = np.eye(128, dtype=np.float32)
    sel2 = np.zeros((2, 128), dtype=ml_dtypes.bfloat16)
    sel2[0, 0:64] = 1.0
    sel2[1, 64:128] = 1.0
    c["c_sel2"] = sel2
    mask2 = np.zeros((128, 2), dtype=ml_dtypes.bfloat16)
    mask2[0:64, 0] = 1.0
    mask2[64:128, 1] = 1.0
    c["c_mask2"] = mask2
    lsc = np.zeros((64, 1), dtype=np.float32)
    lsc[0:32, 0] = BASE_LR / CH
    lsc[32:64, 0] = BASE_WD / CH
    c["c_lrwdscale"] = lsc
    c["c_two"] = np.full((1, 128), 2.0, dtype=ml_dtypes.bfloat16)
    c["c_pool"] = np.tile(np.array([[1e-5, -0.5, 0.5, 1.0 / (FF * D), 1e-8]],
                                   dtype=np.float32), (128, 1))
    selbig = np.zeros((64, 32, 128), dtype=ml_dtypes.bfloat16)
    for p in range(NP):
        for q in range(4):
            idx = q * 8 + p
            selbig[q * 16 + 2 * p, idx, 0:64] = 1.0
            selbig[q * 16 + 2 * p + 1, idx, 64:128] = 1.0
    c["c_selbig"] = selbig.reshape(64, 32 * 128)
    return c


def build_program(L=4096, dbg=(), skip_passB=False):
    NSEG = L // SEG
    NCH = SEG // CH
    nc = bass.Bass("TRN2", target_bir_lowering=False)

    dr = {}
    dr["x"] = nc.dram_tensor("x", [L, D], F32, kind="ExternalInput").ap()
    for nm, shp in [("wq", [D, D]), ("wk", [D, D]), ("wv", [D, D]),
                    ("wo", [D, D]), ("wlr", [32, D]), ("wbeta", [32, D]),
                    ("winit", [64, 16 * 64]), ("woutit", [64, 16 * 64]),
                    ("w1", [FF, D]), ("b1", [FF]), ("w2", [D, FF]), ("b2", [D])]:
        dr[nm] = nc.dram_tensor(nm, shp, F32, kind="ExternalInput").ap()
    consts = host_constants()
    for nm, arr in consts.items():
        dt = BF16 if arr.dtype == ml_dtypes.bfloat16 else F32
        dr[nm] = nc.dram_tensor(nm, list(arr.shape), dt, kind="ExternalInput").ap()
    dr["out"] = nc.dram_tensor("out", [L, D], F32, kind="ExternalOutput").ap()
    dr["s_sp"] = nc.dram_tensor("s_sp", [L, D], BF16, kind="Internal").ap()
    dbg_dr = {}
    for nm, shp, dt in dbg:
        dbg_dr[nm] = nc.dram_tensor(nm, shp, dt, kind="ExternalOutput").ap()

    plan = Plan()

    import contextlib
    with contextlib.ExitStack() as st:
        def T(*args, **kw):
            return st.enter_context(nc.sbuf_tensor(*args, **kw))

        def PS(*args, **kw):
            return st.enter_context(nc.psum_tensor(*args, **kw))

        ident = T("ident", [128, 128], BF16)
        identf = T("identf", [128, 128], F32)
        sel2 = T("sel2", [2, 128], BF16)
        mask2 = T("mask2", [128, 2], BF16)
        lrwdscale = T("lrwdscale", [64, 1], F32)
        two_row = T("two_row", [1, 128], BF16)
        cpool = T("cpool", [128, 8], F32)
        nc.const_aps.register(F32, 1e-5, cpool[:, 0:1])
        nc.const_aps.register(F32, -0.5, cpool[:, 1:2])
        nc.const_aps.register(F32, 0.5, cpool[:, 2:3])
        nc.const_aps.register(F32, 1.0 / (FF * D), cpool[:, 3:4])
        nc.const_aps.register(F32, 1e-8, cpool[:, 4:5])
        xt = T("xt", [128, 4, D], F32)
        outt = T("outt", [128, 2, D], F32)
        abf = T("abf", [128, 76 * 1024], BF16)
        af32 = T("af32", [128, 5 * 1024], F32)
        pb = [PS("pb0", [128, 512], F32), PS("pb1", [128, 512], F32),
              PS("pb2", [128, 512], F32), PS("pb3", [128, 512], F32),
              PS("pb4", [128, 512], F32), PS("pb5", [128, 1024], BF16),
              PS("pb6", [128, 1024], BF16), PS("pb7", [128, 512], F32)]
        sems = {nm: st.enter_context(nc.semaphore("sem_" + nm))
                for nm in Plan.SEMS}
        block = st.enter_context(nc.Block())
        tb = dict(ident=ident, identf=identf, sel2=sel2, mask2=mask2,
                  lrwdscale=lrwdscale, two_row=two_row, cpool=cpool,
                  xt=xt, outt=outt, abf=abf, af32=af32, pb=pb)

        emit_all(nc, plan, dr, dbg_dr, tb, L, NSEG, NCH, skip_passB)

        @block.tensor
        def _(t):
            emit_engine(plan, "t", t, sems)

        @block.vector
        def _(v):
            emit_engine(plan, "v", v, sems)

        @block.scalar
        def _(s):
            emit_engine(plan, "s", s, sems)

        @block.gpsimd
        def _(g):
            emit_engine(plan, "g", g, sems)

        @block.sync
        def _(y):
            emit_engine(plan, "y", y, sems)

    return nc


class Arena:

    def __init__(self, tensor, n_elems):
        self.t = tensor
        self.n = n_elems
        self.off = 0
        self.marks = {}
        self.slots = {}

    def alloc(self, name, shape):
        n = int(np.prod(shape[1:]))
        assert self.off + n <= self.n, f"arena overflow at {name}: {self.off}+{n}>{self.n}"
        base = self.off
        self.off += n
        self.slots[name] = (base, shape)
        return self.view(name)

    def view(self, name):
        base, shape = self.slots[name]
        ap = self.t[:shape[0], base:base + int(np.prod(shape[1:]))]
        if len(shape) > 2:
            pat = "p (" + " ".join(f"d{i}" for i in range(1, len(shape))) + ") -> p " + \
                  " ".join(f"d{i}" for i in range(1, len(shape)))
            kw = {f"d{i}": shape[i] for i in range(2, len(shape))}
            ap = ap.rearrange(pat, **kw)
        return ap

    def mark(self, tag):
        self.marks[tag] = self.off

    def reset_to(self, tag):
        self.off = self.marks[tag]



Z0 = ("v", 0)


DEBUG_ARENAS = {}


def emit_all(nc, plan, dr, dbg_dr, tb, L, NSEG, NCH, skip_passB=False):
    def P(eng, fn, waits=(), dma=False):
        if dma:
            sem_name = f"d{eng}{plan.dma_i[eng] % plan.NROT}"
            tok = plan.op(eng, fn, waits=waits, inc=True, dma=True)
            return (sem_name, tok)
        tok = plan.op(eng, fn, waits=waits, inc=True, dma=False)
        return (eng, tok)

    pb = tb["pb"]
    A = Arena(tb["abf"], 76 * 1024)
    A4 = Arena(tb["af32"], 5 * 1024)
    DEBUG_ARENAS["A"] = A
    DEBUG_ARENAS["A4"] = A4
    ident = tb["ident"]
    identf = tb["identf"]

    b2bf = A.alloc("b2bf", [1, 1024])
    A.mark("postperm")
    selbig = A.alloc("selbig", [64, 32, 128])
    wqT = A.alloc("wqT", [128, 8, 1024])
    wkT = A.alloc("wkT", [128, 8, 1024])
    wvT = A.alloc("wvT", [128, 8, 1024])
    woT = A.alloc("woT", [128, 8, 1024])
    wlrbT = A.alloc("wlrbT", [128, 8, 64])
    winT = A.alloc("winT", [128, 8, 64])
    wout = A.alloc("wout", [128, 8, 64])
    woutT = A.alloc("woutT", [128, 8, 64])
    A.mark("postAW")
    wnat = A.alloc("wnat", [128, 8, 1024])
    winit_nat = A.alloc("winit_nat", [64, 1024])
    woutit_nat = A.alloc("woutit_nat", [64, 1024])
    wlrb_nat = A.alloc("wlrb_nat", [64, 1024])
    A.reset_to("postAW")

    b1nat = A4.alloc("b1nat", [32, 128])
    b1cols = A4.alloc("b1cols", [128, 32])
    lrwd_f = A4.alloc("lrwd_f", [64, 512])
    lrwdc = A4.alloc("lrwdc", [64, 8])
    lrwdB = A4.alloc("lrwdB", [128, 4, 8, 8])
    SUMS = A4.alloc("SUMS", [128, 16])
    RECIP = A4.alloc("RECIP", [128, 16])
    CLI = A4.alloc("CLI", [128, 8])
    CLO = A4.alloc("CLO", [128, 8])
    nmq = A4.alloc("nmq", [128, 1])
    nmk = A4.alloc("nmk", [128, 1])
    bnst = A4.alloc("bnst", [128, 12])
    mv = A4.alloc("mv", [128, 2])
    rstd = A4.alloc("rstd", [128, 1])
    lntmp = A4.alloc("lntmp", [128, 1])
    xt = tb["xt"]
    outt = tb["outt"]

    pb5w = pb[5].ap().rearrange("p (a b) -> p a b", b=128)
    pb5v = pb[5].ap().rearrange("p (a b) -> p a b", b=64)
    pb6w = pb[6].ap().rearrange("p (a b) -> p a b", b=128)
    pb6v = pb[6].ap().rearrange("p (a b) -> p a b", b=64)
    pb7v = pb[7].ap().rearrange("p (a b) -> p a b", b=64)

    dstmap = {"c_ident": ident, "c_identf": identf, "c_sel2": tb["sel2"],
              "c_mask2": tb["mask2"], "c_lrwdscale": tb["lrwdscale"],
              "c_two": tb["two_row"]}
    cy_toks = []
    for nm, dst in dstmap.items():
        cy_toks.append(P("y", lambda e, dst=dst, nm=nm: e.dma_start(dst[:], dr[nm]),
                         dma=True))
    cy_toks.append(P("y", lambda e: e.dma_start(tb["cpool"][:, 0:5], dr["c_pool"]),
                    dma=True))
    cy_toks.append(P("y", lambda e: e.dma_start(selbig.rearrange("p a b -> p (a b)"),
                                                dr["c_selbig"]), dma=True))
    cy_toks.append(P("y", lambda e: e.dma_start(b1nat[:],
                                                dr["b1"].rearrange("(a b) -> a b", b=128)),
                    dma=True))
    cg_toks2 = [
        P("g", lambda e: e.dma_start(winit_nat[:], dr["winit"]), dma=True),
        P("g", lambda e: e.dma_start(woutit_nat[:], dr["woutit"]), dma=True),
        P("g", lambda e: e.dma_start(wlrb_nat[0:32, :], dr["wlr"]), dma=True),
        P("g", lambda e: e.dma_start(wlrb_nat[32:64, :], dr["wbeta"]), dma=True),
    ]
    cg_toks = [P("g", lambda e: e.dma_start(b2bf[:],
                                            dr["b2"].rearrange("(a b) -> a b", a=1)),
                dma=True)]

    t_tok = P("t", lambda e: e.transpose(pb[7][0:128, 0:32], b1nat[:],
                                         identf[0:32, 0:32]),
              waits=cy_toks)
    tok_b1 = P("v", lambda e: e.tensor_copy(b1cols[:], pb[7][0:128, 0:32]),
               waits=[t_tok])

    for h in range(16):
        p_, hh = h // 2, h % 2
        sl = slice(h * 64, (h + 1) * 64)
        hsl = slice(hh * 64, (hh + 1) * 64)
        t_tok = P("t", (lambda e, sl=sl, p_=p_, hh=hh, hsl=hsl:
                        e.transpose(pb5v[hsl, p_, :], winit_nat[:, sl],
                                    ident[0:64, 0:64], tile_position=(0, 64 * hh))),
                  waits=cg_toks + cg_toks2 + cy_toks + [tok_b1])
        t_tok = P("t", (lambda e, sl=sl, p_=p_, hh=hh, hsl=hsl:
                        e.transpose(pb5v[hsl, 8 + p_, :], woutit_nat[:, sl],
                                    ident[0:64, 0:64], tile_position=(0, 64 * hh))))
        t_tok = P("t", (lambda e, sl=sl, p_=p_, hh=hh, hsl=hsl:
                        e.matmul(pb7v[hsl, p_, :], ident[0:64, 0:64],
                                 woutit_nat[:, sl], start=True, stop=True,
                                 skip_group_check=True, tile_position=(0, 64 * hh))))
    P("v", lambda e: e.tensor_copy(winT[:], pb5v[:, 0:8, :]), waits=[t_tok])
    P("v", lambda e: e.tensor_copy(woutT[:], pb5v[:, 8:16, :]))
    tok_state0 = P("v", lambda e: e.tensor_copy(wout[:], pb7v[:, 0:8, :]))

    for fc in range(8):
        t_tok = P("t", (lambda e, fc=fc:
                        e.transpose(pb6v[0:128, 8 + fc, :],
                                    wlrb_nat[:, fc * 128:(fc + 1) * 128],
                                    ident[0:64, 0:64])),
                  waits=[tok_state0])
    tok_evac = P("v", lambda e: e.tensor_copy(wlrbT[:], pb6v[:, 8:16, :]),
                 waits=[t_tok])

    for wname, wT in (("wq", wqT), ("wk", wkT), ("wv", wvT), ("wo", woT)):
        wnat_free = ("t", plan.cnt["t"])
        tok_wload = P("g", (lambda e, wname=wname:
                            e.dma_start(wnat[:],
                                        dr[wname].rearrange("(c p) f -> p c f", p=128))),
                      waits=[wnat_free], dma=True)
        for fc in range(8):
            for c in range(8):
                t_tok = P("t", (lambda e, fc=fc, c=c:
                                e.transpose(pb5w[:, c, :],
                                            wnat[:, c, fc * 128:(fc + 1) * 128],
                                            ident[:])),
                          waits=[tok_wload, tok_evac])
            tok_evac = P("v", (lambda e, wT=wT, fc=fc:
                               e.tensor_copy(wT[:, fc, :],
                                             pb5w.rearrange("p a b -> p (a b)"))),
                         waits=[t_tok])

    qkv = A.alloc("qkv", [128, 8, NCH, 192])
    xn_seg = A.alloc("xn_seg", [128, 8, SEG])
    o_seg = A.alloc("o_seg", [128, 8, SEG])
    s_seg = A.alloc("s_seg", [128, 4, 1024])
    Eq = A.alloc("Eq", [128, 8, 64])
    Ek = A.alloc("Ek", [128, 8, 64])
    p_q = A.alloc("p_q", [128, 8, 64])
    p_li = A.alloc("p_li", [128, 8, 64])
    p_lo = A.alloc("p_lo", [128, 8, 64])
    m_sb = A.alloc("m_sb", [128, 8, 64])
    a_sb = A.alloc("a_sb", [128, 8, 64])
    t_sb = A.alloc("t_sb", [128, 8, 64])
    gz_sb = A.alloc("gz_sb", [128, 8, 64])
    s2_sb = A.alloc("s2_sb", [2, 8, 64])
    rk2 = A.alloc("rk2", [128, 8, 2])
    kT_sb = A.alloc("kT_sb", [64, 8, 128])
    vT_sb = A.alloc("vT_sb", [64, 8, 128])
    pT_sb = A.alloc("pT_sb", [64, 8, 128])
    gzT_sb = A.alloc("gzT_sb", [64, 8, 128])
    xn_tok = A.alloc("xn_tok", [128, 2, 1024])
    lrwdc_bf = A.alloc("lrwdc_bf", [64, 8])
    A.mark("passA_end")

    pbv = [pb[i].ap().rearrange("p (a b) -> p a b", b=64) for i in range(5)]
    views = dict(qkv=qkv, xn_seg=xn_seg, o_seg=o_seg, s_seg=s_seg, Eq=Eq, Ek=Ek,
                 p_q=p_q, p_li=p_li, p_lo=p_lo, m_sb=m_sb, a_sb=a_sb, t_sb=t_sb,
                 gz_sb=gz_sb, s2_sb=s2_sb, rk2=rk2, kT_sb=kT_sb, vT_sb=vT_sb,
                 pT_sb=pT_sb, gzT_sb=gzT_sb, winT=winT, wout=wout, woutT=woutT,
                 SUMS=SUMS, RECIP=RECIP, CLI=CLI, CLO=CLO, nmq=nmq, nmk=nmk,
                 lrwdB=lrwdB, pbv=pbv, pb5w=pb5w, pb6w=pb6w,
                 wqT=wqT, wkT=wkT, wvT=wvT, woT=woT, wlrbT=wlrbT,
                 lrwd_f=lrwd_f, lrwdc=lrwdc, bnst=bnst, mv=mv, rstd=rstd,
                 lntmp=lntmp, xn_tok=xn_tok, selbig=selbig,
                 lrwdc_bf=lrwdc_bf)

    ck = {
        "upd_win": tok_state0, "upd_wout": tok_state0, "wt_evac": tok_state0,
        "b0": Z0, "b1": Z0, "b2": Z0, "b3": Z0, "b4": Z0,
        "b5": tok_evac, "b6": tok_evac, "b7": tok_b1,
        "eq": Z0, "ek": Z0, "pq": Z0, "pli": Z0, "plo": Z0, "m": Z0, "a": Z0,
        "gz": Z0, "rk2": Z0, "s2": Z0, "kt": Z0, "vt": Z0, "pt": Z0, "gzt": Z0,
        "nm": Z0, "qkv": Z0, "xnseg": Z0, "oseg": Z0, "sseg": Z0, "lrwdf": Z0,
        "lrwdB_w": Z0, "xt": [Z0, Z0, Z0, Z0], "xntok": [Z0, Z0], "mv": Z0,
        "sdma": Z0,
        "o_evac": ("s", 0), "qkv_ready": ("s", 0), "lrwdB_ready": ("v", 0),
    }

    for seg in range(NSEG):
        emit_segment(plan, P, dr, tb, views, ck, seg, NSEG, NCH)

    barA = plan.barrier_tokens()
    bar_toks = [(sm, c) for sm, c in barA.items() if c > 0]
    if not skip_passB:
        A.reset_to("postperm")
        emit_passB(nc, plan, P, dr, dbg_dr, tb, A, A4, bar_toks, L,
                   b2bf, b1cols, ident, identf)

    fin = [(sm, plan.cnt[sm]) for sm in Plan.SEMS
           if sm.startswith("d") and plan.cnt[sm] > 0]
    plan.op("y", lambda e: None, waits=fin, inc=False)


def emit_segment(plan, P, dr, tb, V, ck, seg, NSEG, NCH):
    pb = tb["pb"]
    pbv = V["pbv"]
    pb5w, pb6w = V["pb5w"], V["pb6w"]
    ident = tb["ident"]
    xt = tb["xt"]
    qkv, xn_seg, o_seg, s_seg = V["qkv"], V["xn_seg"], V["o_seg"], V["s_seg"]
    A4v = {k: V[k] for k in ("SUMS", "RECIP", "CLI", "CLO", "nmq", "nmk", "lrwdB")}

    for tsub in range(4):
        gidx = seg * 4 + tsub
        par = tsub % 2
        tok_x = P("y", (lambda e, gidx=gidx, par=par:
                        e.dma_start(xt[:, par, :],
                                    dr["x"][gidx * 128:(gidx + 1) * 128, :])),
                  waits=[ck["xt"][par]], dma=True)
        tok_bn = P("v", (lambda e, par=par:
                         e.bn_stats(V["bnst"][:, 0:6], xt[:, par, 0:512])),
                   waits=[tok_x, ck["mv"]])
        tok_bn = P("v", (lambda e, par=par:
                         e.bn_stats(V["bnst"][:, 6:12], xt[:, par, 512:1024])))
        tok_mv = P("v", lambda e: e.bn_aggr(V["mv"][:], V["bnst"][:, 0:12]),
                   waits=[tok_bn])
        tok_ln = P("s", lambda e: e.activation(V["lntmp"][:], V["mv"][:, 1:2],
                                               ACT.Ln, bias=1e-5),
                   waits=[tok_mv])
        tok_rstd = P("s", lambda e: e.activation(V["rstd"][:], V["lntmp"][:],
                                                 ACT.Exp, scale=-0.5),
                     waits=[tok_ln])
        ck["mv"] = tok_rstd
        tok_xntok = P("v", (lambda e, par=par:
                            e.tensor_scalar(V["xn_tok"][:, par, :], xt[:, par, :],
                                            V["mv"][:, 0:1], V["rstd"][:],
                                            op0=OP.subtract, op1=OP.mult)),
                      waits=[tok_rstd, tok_mv, ck["xntok"][par]])
        ck["xt"][par] = tok_xntok
        for fc in range(8):
            t_tok = P("t", (lambda e, par=par, fc=fc:
                            e.transpose(pb5w[:, fc, :],
                                        V["xn_tok"][:, par, fc * 128:(fc + 1) * 128],
                                        ident[:])),
                      waits=[tok_xntok, ck["b5"]])
        ck["xntok"][par] = t_tok
        tok_ev = P("v", (lambda e, tsub=tsub:
                         e.tensor_copy(xn_seg[:, :, tsub * 128:(tsub + 1) * 128],
                                       pb5w[:, :, :])),
                   waits=[t_tok, ck["xnseg"]])
        ck["b5"] = tok_ev
    tok_xn = tok_ev

    sev = [("s", 0), ("s", 0)]
    bank_free = [ck["b0"], ck["b1"]]
    pp = 0
    for wTname, coff, actf in (("wqT", 0, ACT.Silu), ("wkT", 64, ACT.Silu),
                               ("wvT", 128, ACT.Copy)):
        wT = V[wTname]
        for m in range(8):
            bank = pb[pp % 2]
            for fc in range(8):
                t_tok = P("t", (lambda e, wT=wT, m=m, fc=fc, bank=bank:
                                e.matmul(bank[:, :],
                                         wT[:, fc, m * 128:(m + 1) * 128],
                                         xn_seg[:, fc, :],
                                         start=(fc == 0), stop=(fc == 7),
                                         skip_group_check=True)),
                          waits=[tok_xn, sev[pp % 2], bank_free[pp % 2]])
            bview = bank.ap().rearrange("p (a b) -> p a b", b=64)
            tok_evac = P("s", (lambda e, bview=bview, m=m, coff=coff, actf=actf:
                               e.activation(qkv[:, m, :, coff:coff + 64],
                                            bview[:, :, :], actf)),
                         waits=[t_tok, ck["qkv"]])
            sev[pp % 2] = tok_evac
            pp += 1
    ck["b0"] = sev[0]
    ck["b1"] = sev[1]
    ck["qkv_ready"] = tok_evac
    for fc in range(8):
        t_tok = P("t", (lambda e, fc=fc:
                        e.matmul(pb[2][0:64, :], V["wlrbT"][:, fc, :],
                                 xn_seg[:, fc, :], start=(fc == 0),
                                 stop=(fc == 7), skip_group_check=True)),
                  waits=[tok_xn, ck["b2"]])
    tok_sg = P("s", lambda e: e.activation(V["lrwd_f"][:], pb[2][0:64, :],
                                           ACT.Sigmoid),
               waits=[t_tok, ck["lrwdf"]])
    ck["b2"] = tok_sg
    tok_rd = P("v", (lambda e:
                     e.reduce_sum(V["lrwdc"][:],
                                  V["lrwd_f"].rearrange("p (a b) -> p a b", b=64),
                                  axis=AX.X)),
              waits=[tok_sg])
    ck["lrwdf"] = tok_rd
    tok_lc = P("v", lambda e: e.tensor_scalar(V["lrwdc_bf"][:], V["lrwdc"][:],
                                              tb["lrwdscale"][:], None,
                                              op0=OP.mult), waits=[tok_rd])
    pb3b = pb[3].ap().rearrange("p (a b) -> p a b", b=8)
    for idx in range(32):
        t_tok = P("t", (lambda e, idx=idx:
                        e.matmul(pb3b[:, idx, :],
                                 V["selbig"][:, idx, :],
                                 V["lrwdc_bf"][:], start=True, stop=True,
                                 skip_group_check=True)),
                  waits=[tok_lc, ck["b3"], ck["lrwdB_w"]])
    tok_lB = P("v", (lambda e:
                     e.tensor_copy(V["lrwdB"].rearrange("p a b c -> p (a b c)"),
                                   pb[3][:, 0:256])),
               waits=[t_tok])
    ck["b3"] = tok_lB
    ck["lrwdB_ready"] = tok_lB
    ck["xnseg"] = ("t", plan.cnt["t"])

    for ci in range(NCH):
        emit_scan_chunk(plan, P, tb, V, ck, ci)
    ck["qkv"] = ("t", plan.cnt["t"])

    tok_sev = ("s", 0)
    for tsub in range(4):
        for half in range(2):
            bank = pb[2 + half]
            bfree = ck["b2"] if half == 0 else ck["b3"]
            for fc in range(8):
                t_tok = P("t", (lambda e, tsub=tsub, half=half, fc=fc, bank=bank:
                                e.matmul(bank[:, :],
                                         o_seg[:, fc, tsub * 128:(tsub + 1) * 128],
                                         V["woT"][:, fc, half * 512:(half + 1) * 512],
                                         start=(fc == 0), stop=(fc == 7),
                                         skip_group_check=True)),
                          waits=[ck["o_evac"], bfree])
            tok_sev = P("s", (lambda e, tsub=tsub, half=half, bank=bank:
                              e.activation(s_seg[:, tsub, half * 512:(half + 1) * 512],
                                           bank[:, :], ACT.Copy)),
                        waits=[t_tok, ck["sseg"]])
            if half == 0:
                ck["b2"] = tok_sev
            else:
                ck["b3"] = tok_sev
    ck["oseg"] = t_tok
    tok_sdma = P("g", (lambda e, seg=seg:
                       e.dma_start(dr["s_sp"][seg * SEG:(seg + 1) * SEG, :]
                                   .rearrange("(a p) f -> p a f", p=128),
                                   s_seg[:])),
                 waits=[tok_sev, ck["sdma"]], dma=True)
    ck["sseg"] = tok_sdma
    ck["sdma"] = tok_sdma


def emit_scan_chunk(plan, P, tb, V, ck, ci):
    pb = tb["pb"]
    pbv = V["pbv"]
    pb5w, pb6w = V["pb5w"], V["pb6w"]
    ident = tb["ident"]
    qkv, winT, wout, woutT = V["qkv"], V["winT"], V["wout"], V["woutT"]
    Eq, Ek, p_q, p_li, p_lo = V["Eq"], V["Ek"], V["p_q"], V["p_li"], V["p_lo"]
    m_sb, a_sb, t_sb, gz_sb = V["m_sb"], V["a_sb"], V["t_sb"], V["gz_sb"]
    s2_sb, rk2 = V["s2_sb"], V["rk2"]
    kT_sb, vT_sb, pT_sb, gzT_sb = V["kT_sb"], V["vT_sb"], V["pT_sb"], V["gzT_sb"]
    SUMS, RECIP, CLI, CLO = V["SUMS"], V["RECIP"], V["CLI"], V["CLO"]
    nmq, nmk, lrwdB = V["nmq"], V["nmk"], V["lrwdB"]
    pb5WT = pb[5].ap().rearrange("p (a b) -> p a b", b=64)

    def heads():
        for p in range(NP):
            for hh in range(2):
                yield p, hh, slice(hh * 64, (hh + 1) * 64), (0 if hh == 0 else 64)

    for p, hh, hsl, off in heads():
        t_tok = P("t", (lambda e, p=p, hh=hh, hsl=hsl, off=off:
                        e.matmul(pbv[0][hsl, p, :], winT[hsl, p, :],
                                 qkv[hsl, p, ci, 0:64], start=True, stop=True,
                                 skip_group_check=True, tile_position=(off, off))),
                  waits=[ck["upd_win"], ck["qkv_ready"], ck["b0"], ck["nm"]])
        t_tok = P("t", (lambda e, p=p, hh=hh, hsl=hsl, off=off:
                        e.matmul(pbv[1][hsl, p, :], winT[hsl, p, :],
                                 qkv[hsl, p, ci, 64:128], start=True, stop=True,
                                 skip_group_check=True, tile_position=(off, off))),
                  waits=[ck["b1"]])
    tok_z = t_tok

    for p in range(NP):
        t_tok = P("t", (lambda e, p=p:
                        e.transpose(pb5w[0:64, p, :], qkv[:, p, ci, 64:128],
                                    ident[:])),
                  waits=[ck["b5"]])
    tok_trk = t_tok
    for p in range(NP):
        t_tok = P("t", (lambda e, p=p:
                        e.transpose(pb6w[0:64, p, :], qkv[:, p, ci, 128:192],
                                    ident[:])),
                  waits=[ck["b6"]])
    tok_trv = t_tok
    tok_kt = P("v", lambda e: e.tensor_copy(kT_sb[:], pb5w[0:64, 0:8, :]),
               waits=[tok_trk, ck["kt"]])
    ck["b5"] = tok_kt
    tok_vt = P("s", lambda e: e.activation(vT_sb[:], pb6w[0:64, 0:8, :], ACT.Copy),
               waits=[tok_trv, ck["vt"]])
    ck["b6"] = tok_vt

    for p, hh, hsl, off in heads():
        t_tok = P("t", (lambda e, p=p, hh=hh, hsl=hsl, off=off:
                        e.matmul(pbv[2][hsl, p, :], woutT[hsl, p, :],
                                 qkv[hsl, p, ci, 128:192], start=True, stop=True,
                                 skip_group_check=True, tile_position=(off, off))),
                  waits=[ck["wt_evac"], ck["b2"]])
    tok_amm = t_tok

    tok_nmq = P("v", lambda e: e.reduce_max(nmq[:], pb[0][:, :], axis=AX.X,
                                            negate=True),
                waits=[tok_z, ck["nm"]])
    tok_nmk = P("v", lambda e: e.reduce_max(nmk[:], pb[1][:, :], axis=AX.X,
                                            negate=True))
    tok_eq = P("s", lambda e: e.activation(Eq[:], pb[0][:, :], ACT.Exp,
                                           bias=nmq[:]),
               waits=[tok_nmq, ck["eq"]])
    tok_ek = P("s", lambda e: e.activation(Ek[:], pb[1][:, :], ACT.Exp,
                                           bias=nmk[:]),
               waits=[tok_nmk, ck["ek"]])
    ck["b0"] = tok_eq
    ck["b1"] = tok_ek
    ck["nm"] = tok_ek
    tok_sums = P("v", lambda e: e.reduce_sum(SUMS[:, 0:8], Eq[:], axis=AX.X),
                 waits=[tok_eq])
    tok_sums = P("v", lambda e: e.reduce_sum(SUMS[:, 8:16], Ek[:], axis=AX.X),
                 waits=[tok_ek])
    tok_rec = P("v", lambda e: e.reciprocal(RECIP[:], SUMS[:]),
                waits=[tok_sums])
    tok_cli = P("v", (lambda e:
                      e.tensor_tensor(CLI[:], RECIP[:, 8:16],
                                      lrwdB[:, 0, :, ci], OP.mult)),
                waits=[ck["lrwdB_ready"], tok_rec])
    tok_clo = P("v", (lambda e:
                      e.tensor_tensor(CLO[:], RECIP[:, 8:16],
                                      lrwdB[:, 1, :, ci], OP.mult)))
    for p in range(NP):
        tok_pq = P("v", (lambda e, p=p:
                         e.tensor_scalar(p_q[:, p, :], Eq[:, p, :],
                                         RECIP[:, p:p + 1], None, op0=OP.mult)),
                   waits=[ck["pq"], tok_rec])
        tok_pli = P("v", (lambda e, p=p:
                          e.tensor_scalar(p_li[:, p, :], Ek[:, p, :],
                                          CLI[:, p:p + 1], None, op0=OP.mult)),
                    waits=[ck["pli"], tok_clo])
        tok_plo = P("v", (lambda e, p=p:
                          e.tensor_scalar(p_lo[:, p, :], Ek[:, p, :],
                                          CLO[:, p:p + 1], None, op0=OP.mult)),
                    waits=[ck["plo"]])
        tok_rk2 = P("v", (lambda e, p=p:
                          e.tensor_scalar(rk2[:, p, :], tb["mask2"][:],
                                          RECIP[:, 8 + p:9 + p], None,
                                          op0=OP.mult)),
                    waits=[ck["rk2"]])
    tok_eqfree = tok_rk2

    for p, hh, hsl, off in heads():
        t_tok = P("t", (lambda e, p=p, hh=hh, hsl=hsl, off=off:
                        e.matmul(pbv[3][hsl, p, :], wout[hsl, p, :],
                                 p_q[hsl, p, :], start=True, stop=True,
                                 skip_group_check=True, tile_position=(off, off))),
                  waits=[ck["upd_wout"], ck["b3"], tok_pq])
    tok_omm = t_tok
    ck["pq"] = tok_omm
    tok_oev = P("s", (lambda e, ci=ci:
                      e.activation(V["o_seg"][:, :, ci * 64:(ci + 1) * 64],
                                   pbv[3][:, 0:8, :], ACT.Copy)),
                waits=[tok_omm, ck["oseg"]])
    ck["b3"] = tok_oev
    ck["o_evac"] = tok_oev

    tok_aev = P("s", lambda e: e.activation(a_sb[:], pbv[2][:, 0:8, :], ACT.Copy),
                waits=[tok_amm, ck["a"]])
    tok_m = P("v", lambda e: e.tensor_tensor(m_sb[:], Ek[:], pbv[2][:, 0:8, :],
                                             OP.mult),
              waits=[tok_ek, tok_aev, ck["m"]])
    ck["b2"] = tok_m
    ck["eq"] = tok_eqfree
    ck["ek"] = tok_m
    for p in range(NP):
        t_tok = P("t", (lambda e, p=p:
                        e.matmul(pbv[4][0:2, p, :], rk2[:, p, :], m_sb[:, p, :],
                                 start=True, stop=True, skip_group_check=True)),
                  waits=[tok_m, tok_rk2, ck["b4"]])
    tok_s2mm = t_tok
    ck["rk2"] = tok_s2mm
    ck["m"] = tok_s2mm
    tok_s2ev = P("s", lambda e: e.activation(s2_sb[:], pbv[4][0:2, 0:8, :],
                                             ACT.Copy),
                 waits=[tok_s2mm, ck["s2"]])
    tok_sb = P("t", (lambda e:
                     e.matmul(pb[4][:, :], tb["sel2"][:],
                              s2_sb.rearrange("p a b -> p (a b)"),
                              start=True, stop=True, skip_group_check=True)),
              waits=[tok_s2ev])
    ck["s2"] = tok_sb
    tok_t = P("v", lambda e: e.tensor_tensor(t_sb[:], pb[4].ap()
                                             .rearrange("p (a b) -> p a b", b=64),
                                             a_sb[:], OP.subtract),
              waits=[tok_sb, tok_aev])
    ck["b4"] = tok_t
    ck["a"] = tok_t
    tok_gz = P("v", lambda e: e.tensor_tensor(gz_sb[:], t_sb[:], p_li[:], OP.mult),
               waits=[tok_pli, tok_t, ck["gz"]])
    ck["pli"] = tok_gz

    for p in range(NP):
        t_tok = P("t", (lambda e, p=p:
                        e.transpose(pb5w[0:64, p, :], p_lo[:, p, :], ident[:])),
                  waits=[tok_plo, ck["b5"]])
    tok_trp = t_tok
    tok_pt = P("v", lambda e: e.tensor_copy(pT_sb[:], pb5w[0:64, 0:8, :]),
               waits=[tok_trp, ck["pt"]])
    ck["b5"] = tok_pt
    ck["plo"] = tok_trp
    for p in range(NP):
        t_tok = P("t", (lambda e, p=p:
                        e.transpose(pb6w[0:64, p, :], gz_sb[:, p, :], ident[:])),
                  waits=[tok_gz, ck["b6"]])
    tok_trgz = t_tok
    ck["gz"] = tok_trgz
    tok_gzt = P("s", lambda e: e.activation(gzT_sb[:], pb6w[0:64, 0:8, :],
                                            ACT.Copy),
                waits=[tok_trgz, ck["gzt"]])
    ck["b6"] = tok_gzt

    for p, hh, hsl, off in heads():
        t_tok = P("t", (lambda e, p=p, hh=hh, hsl=hsl, off=off:
                        e.matmul(pbv[0][hsl, p, :],
                                 pT_sb[0:64, p, hh * 64:(hh + 1) * 64],
                                 vT_sb[0:64, p, hh * 64:(hh + 1) * 64],
                                 start=True, stop=True, skip_group_check=True,
                                 tile_position=(0, off))),
                  waits=[tok_pt, tok_vt, ck["b0"]])
    tok_g11 = t_tok
    for p, hh, hsl, off in heads():
        t_tok = P("t", (lambda e, p=p, hh=hh, hsl=hsl, off=off:
                        e.matmul(pbv[1][hsl, p, :],
                                 kT_sb[0:64, p, hh * 64:(hh + 1) * 64],
                                 gzT_sb[0:64, p, hh * 64:(hh + 1) * 64],
                                 start=True, stop=True, skip_group_check=True,
                                 tile_position=(0, off))),
                  waits=[tok_kt, tok_gzt, ck["b1"]])
    tok_g13 = t_tok
    ck["kt"] = tok_g13
    ck["vt"] = tok_g11
    ck["pt"] = tok_g11
    ck["gzt"] = tok_g13

    for p in range(NP):
        tok_uo = P("v", (lambda e, p=p:
                         e.scalar_tensor_tensor(wout[:, p, :], wout[:, p, :],
                                                lrwdB[:, 3, p, ci:ci + 1],
                                                pbv[0][:, p, :], op0=OP.mult,
                                                op1=OP.add)),
                   waits=[tok_g11, tok_omm, ck["upd_wout"], ck["lrwdB_ready"]])
        tok_ui = P("v", (lambda e, p=p:
                         e.scalar_tensor_tensor(winT[:, p, :], winT[:, p, :],
                                                lrwdB[:, 2, p, ci:ci + 1],
                                                pbv[1][:, p, :], op0=OP.mult,
                                                op1=OP.subtract)),
                   waits=[tok_g13, tok_z, ck["upd_win"]])
    ck["upd_win"] = tok_ui
    ck["upd_wout"] = tok_uo
    ck["b0"] = tok_uo
    ck["b1"] = tok_ui

    for p, hh, hsl, off in heads():
        t_tok = P("t", (lambda e, p=p, hh=hh, hsl=hsl, off=off:
                        e.transpose(pb5WT[hsl, p, :], wout[hsl, p, :],
                                    ident[hsl, hsl], tile_position=(off, off))),
                  waits=[tok_uo, ck["b5"], tok_amm])
    tok_trw = t_tok
    tok_wt = P("s", lambda e: e.activation(woutT[:], pb5WT[:, 0:8, :], ACT.Copy),
               waits=[tok_trw])
    ck["b5"] = tok_wt
    ck["wt_evac"] = tok_wt


def emit_passB(nc, plan, P, dr, dbg_dr, tb, A, A4, bar_toks, L,
               b2bf, b1cols, ident, identf):
    pb = tb["pb"]
    xt = tb["xt"]
    outt = tb["outt"]
    pb5w = pb[5].ap().rearrange("p (a b) -> p a b", b=128)
    NBT = L // SEGB

    for e in Plan.ENGS:
        plan.op(e, lambda _e: None, waits=bar_toks, inc=False)

    w1T = A.alloc("w1T", [128, 8, FF])
    w2T = A.alloc("w2T", [128, 32, 1024])
    hT = A.alloc("hT", [128, 2, SEGB])
    xnB = A.alloc("xnB", [128, 8, SEGB])
    w12scr = A.alloc("w12scr", [128, 1024])
    sgA = A.alloc("sgA", [128, 1024])
    s_rd = A.alloc("s_rd", [128, 2, 1024])
    g_sb = A.alloc("g_sb", [128, 2, 1024])
    xn_tokB = A.alloc("xn_tokB", [128, 2, 1024])

    wstream = A4.alloc("wstream", [128, 2, 1024])
    acc1 = A4.alloc("accw1", [128, 1])
    acc2 = A4.alloc("accw2", [128, 1])
    tabs = A4.alloc("tabsw", [128, 1])
    tot = A4.alloc("totw", [128, 1])
    inv1 = A4.alloc("invw1", [128, 1])
    inv2 = A4.alloc("invw2", [128, 1])
    bnst = A4.alloc("bnstB", [128, 12])
    mv = A4.alloc("mvB", [128, 2])
    rstd = A4.alloc("rstdB", [128, 1])
    lntmp = A4.alloc("lntmpB", [128, 1])

    def w2_tile_src(c):
        rt, cc = c // 4, c % 4
        return dr["w2"][rt * 128:(rt + 1) * 128, cc * 1024:(cc + 1) * 1024]

    ws_free = [Z0, Z0]
    tabs_free = Z0
    for wname, acc in (("w1", acc1), ("w2", acc2)):
        for c in range(32):
            par = c % 2
            src = (dr["w1"][c * 128:(c + 1) * 128, :] if wname == "w1"
                   else w2_tile_src(c))
            tok_l = P("y", (lambda e, src=src, par=par:
                            e.dma_start(wstream[:, par, :], src)),
                      waits=[ws_free[par]], dma=True)
            tok_tb = P("v", (lambda e, par=par:
                             e.reduce_sum(tabs[:], wstream[:, par, :], axis=AX.X,
                                          apply_absolute_value=True)),
                       waits=[tok_l, tabs_free])
            if c == 0:
                tok_a = P("v", (lambda e, acc=acc: e.tensor_copy(acc[:], tabs[:])),
                          waits=[tok_tb])
            else:
                tok_a = P("v", (lambda e, acc=acc:
                                e.tensor_tensor(acc[:], acc[:], tabs[:], OP.add)),
                          waits=[tok_tb])
            ws_free[par] = tok_a
            tabs_free = tok_a
    ones_col = A4.alloc("ones_col", [128, 1])
    ones_row = A4.alloc("ones_row", [1, 128])
    tok_o1 = P("v", lambda e: e.memset(ones_col[:], 1.0))
    tok_o2 = P("v", lambda e: e.memset(ones_row[:], 1.0))
    for acc, inv in ((acc1, inv1), (acc2, inv2)):
        t_tok = P("t", (lambda e, acc=acc:
                        e.matmul(pb[7][0:1, 0:1], ones_col[:], acc[:],
                                 start=True, stop=True, skip_group_check=True)),
                  waits=[ws_free[0], ws_free[1], tok_o1, mm1_free_ph := Z0])
        tok_cp = P("v", lambda e: e.tensor_copy(tot[0:1, :], pb[7][0:1, 0:1]),
                   waits=[t_tok])
        t_tok = P("t", (lambda e:
                        e.matmul(pb[7][0:128, 1:2], ones_row[:], tot[0:1, :],
                                 start=True, stop=True, skip_group_check=True)),
                  waits=[tok_cp, tok_o2])
        tok_t1 = P("v", (lambda e: e.tensor_scalar(tabs[:], pb[7][0:128, 1:2],
                                                   1.0 / (FF * D), None,
                                                   op0=OP.mult)),
                   waits=[t_tok, tabs_free])
        tok_t2 = P("v", (lambda e: e.tensor_scalar(tabs[:], tabs[:], 1e-8, None,
                                                   op0=OP.max)), waits=[tok_t1])
        tok_i = P("v", (lambda e, inv=inv: e.reciprocal(inv[:], tabs[:])),
                  waits=[tok_t2])
        tabs_free = tok_i

    tok_trev = Z0
    for wname, wT, inv in (("w1", w1T, inv1), ("w2", w2T, inv2)):
        for c in range(32):
            par = c % 2
            src = (dr["w1"][c * 128:(c + 1) * 128, :] if wname == "w1"
                   else w2_tile_src(c))
            tok_l = P("y", (lambda e, src=src, par=par:
                            e.dma_start(wstream[:, par, :], src)),
                      waits=[ws_free[par]], dma=True)
            tok_sa = P("s", (lambda e, par=par, inv=inv:
                             e.activation(w12scr[:], wstream[:, par, :], ACT.Sign,
                                          bias=-0.5, scale=inv[:])),
                       waits=[tok_l, tok_i, tok_trev])
            tok_sb_ = P("s", (lambda e, par=par, inv=inv:
                              e.activation(sgA[:], wstream[:, par, :], ACT.Sign,
                                           bias=0.5, scale=inv[:])))
            ws_free[par] = tok_sb_
            tok_w = P("v", lambda e: e.tensor_tensor(w12scr[:], w12scr[:], sgA[:],
                                                     OP.add),
                      waits=[tok_sb_])
            for j in range(8):
                t_tok = P("t", (lambda e, j=j:
                                e.transpose(pb5w[:, j, :],
                                            w12scr[:, j * 128:(j + 1) * 128],
                                            ident[:])),
                          waits=[tok_w, tok_trev])
            if wname == "w1":
                dst = w1T[:, :, c * 128:(c + 1) * 128]
            else:
                rt, cc = c // 4, c % 4
                dst = w2T[:, cc * 8:(cc + 1) * 8, rt * 128:(rt + 1) * 128]
            tok_trev = P("v", (lambda e, dst=dst: e.tensor_copy(dst, pb5w[:, :, :])),
                         waits=[t_tok])

    xt_free = [Z0, Z0, Z0, Z0]
    h_free = [Z0, Z0]
    bank_free = {i: tok_trev for i in range(4)}
    mm1_free = {4: tok_trev, 7: ("v", 0)}
    g_free = [Z0, Z0]
    srd_free = Z0
    out_free = [Z0, Z0]
    mvtok = Z0
    xntokB_free = [Z0, Z0]
    tok_xnB = Z0
    for bt in range(NBT):
        tok_srd = P("y", (lambda e, bt=bt:
                          e.dma_start(s_rd[:, 0:2, :],
                                      dr["s_sp"][bt * SEGB:(bt + 1) * SEGB, :]
                                      .rearrange("(a p) f -> p a f", p=128))),
                    waits=[srd_free], dma=True)
        for tsub in range(2):
            gidx = bt * 2 + tsub
            slot = gidx % 4
            par = tsub % 2
            tok_x = P("y", (lambda e, gidx=gidx, slot=slot:
                            e.dma_start(xt[:, slot, :],
                                        dr["x"][gidx * 128:(gidx + 1) * 128, :])),
                      waits=[xt_free[slot]], dma=True)
            tok_bn = P("v", (lambda e, slot=slot:
                             e.bn_stats(bnst[:, 0:6], xt[:, slot, 0:512])),
                       waits=[tok_x, mvtok])
            tok_bn = P("v", (lambda e, slot=slot:
                             e.bn_stats(bnst[:, 6:12], xt[:, slot, 512:1024])))
            tok_mv = P("v", lambda e: e.bn_aggr(mv[:], bnst[:, 0:12]),
                       waits=[tok_bn])
            tok_ln = P("s", lambda e: e.activation(lntmp[:], mv[:, 1:2], ACT.Ln,
                                                   bias=1e-5),
                      waits=[tok_mv])
            tok_rstd = P("s", lambda e: e.activation(rstd[:], lntmp[:], ACT.Exp,
                                                     scale=-0.5), waits=[tok_ln])
            mvtok = tok_rstd
            tok_xntok = P("v", (lambda e, slot=slot, par=par:
                                e.tensor_scalar(xn_tokB[:, par, :], xt[:, slot, :],
                                                mv[:, 0:1], rstd[:],
                                                op0=OP.subtract, op1=OP.mult)),
                          waits=[tok_rstd, tok_mv, xntokB_free[par]])
            for fc in range(8):
                t_tok = P("t", (lambda e, par=par, fc=fc:
                                e.transpose(pb5w[:, fc, :],
                                            xn_tokB[:, par, fc * 128:(fc + 1) * 128],
                                            ident[:])),
                          waits=[tok_xntok, tok_trev])
            xntokB_free[par] = t_tok
            tok_xnB = P("v", (lambda e, tsub=tsub:
                              e.tensor_copy(xnB[:, :, tsub * 128:(tsub + 1) * 128],
                                            pb5w[:, :, :])),
                        waits=[t_tok, tok_xnB])
            tok_trev = tok_xnB
        for f1c in range(32):
            par = f1c % 2
            bank = pb[4] if par == 0 else pb[7]
            for fc in range(8):
                t_tok = P("t", (lambda e, f1c=f1c, fc=fc, bank=bank:
                                e.matmul(bank[:, 0:SEGB],
                                         w1T[:, fc, f1c * 128:(f1c + 1) * 128],
                                         xnB[:, fc, :], start=(fc == 0),
                                         stop=(fc == 7), skip_group_check=True)),
                          waits=[tok_xnB, mm1_free[4 if par == 0 else 7]])
            tok_h = P("s", (lambda e, par=par, f1c=f1c, bank=bank:
                            e.activation(hT[:, par, :], bank[:, 0:SEGB], ACT.Gelu,
                                         scale=0.5, bias=b1cols[:, f1c:f1c + 1])),
                      waits=[t_tok, h_free[par]])
            for tsub in range(2):
                for half in range(2):
                    j = tsub * 2 + half
                    t_tok = P("t", (lambda e, f1c=f1c, par=par, tsub=tsub, half=half, j=j:
                                    e.matmul(pb[j][:, :],
                                             hT[:, par, tsub * 128:(tsub + 1) * 128],
                                             w2T[:, f1c, half * 512:(half + 1) * 512],
                                             start=(f1c == 0), stop=False,
                                             skip_group_check=True)),
                              waits=[tok_h, bank_free[j]])
            h_free[par] = t_tok
            mm1_free[4 if par == 0 else 7] = tok_h
        for tsub in range(2):
            for half in range(2):
                j = tsub * 2 + half
                t_tok = P("t", (lambda e, j=j, half=half:
                                e.matmul(pb[j][:, :], tb["two_row"][:],
                                         b2bf[0:1, half * 512:(half + 1) * 512],
                                         start=False, stop=True,
                                         skip_group_check=True)))
                tok_g = P("s", (lambda e, j=j, tsub=tsub, half=half:
                                e.activation(g_sb[:, tsub, half * 512:(half + 1) * 512],
                                             pb[j][:, :], ACT.Silu, scale=0.5)),
                          waits=[t_tok, g_free[tsub]])
                bank_free[j] = tok_g
            gidx = bt * 2 + tsub
            slot = gidx % 4
            opar = gidx % 2
            tok_c1 = P("v", (lambda e, tsub=tsub, opar=opar:
                             e.tensor_tensor(outt[:, opar, :], s_rd[:, tsub, :],
                                             g_sb[:, tsub, :], OP.mult)),
                       waits=[tok_g, tok_srd, out_free[opar]])
            tok_c2 = P("v", (lambda e, slot=slot, opar=opar:
                             e.tensor_tensor(outt[:, opar, :], outt[:, opar, :],
                                             xt[:, slot, :], OP.add)),
                       waits=[tok_c1])
            g_free[tsub] = tok_c2
            xt_free[slot] = tok_c2
            tok_od = P("g", (lambda e, gidx=gidx, opar=opar:
                             e.dma_start(dr["out"][gidx * 128:(gidx + 1) * 128, :],
                                         outt[:, opar, :])),
                       waits=[tok_c2], dma=True)
            out_free[opar] = tok_od
            if tsub == 1:
                srd_free = tok_c1


_NC_CACHE = {}


def _get_program(L=4096):
    if L not in _NC_CACHE:
        _NC_CACHE[L] = build_program(L=L)
    return _NC_CACHE[L]


def kernel(L0_attn_x, L0_attn_Wq, L0_attn_Wk, L0_attn_Wv, L0_attn_Wlr,
           L0_attn_Wbeta, L0_attn_Wo, L0_attn_W_in_init, L0_attn_W_out_init,
           L0_ffn_W1, L0_ffn_b1, L0_ffn_W2, L0_ffn_b2):
    x = np.ascontiguousarray(np.asarray(L0_attn_x, dtype=np.float32))
    B, L, _ = x.shape
    base = {
        "wq": np.ascontiguousarray(np.asarray(L0_attn_Wq, np.float32)),
        "wk": np.ascontiguousarray(np.asarray(L0_attn_Wk, np.float32)),
        "wv": np.ascontiguousarray(np.asarray(L0_attn_Wv, np.float32)),
        "wo": np.ascontiguousarray(np.asarray(L0_attn_Wo, np.float32)),
        "wlr": np.ascontiguousarray(np.asarray(L0_attn_Wlr, np.float32)),
        "wbeta": np.ascontiguousarray(np.asarray(L0_attn_Wbeta, np.float32)),
        "winit": np.ascontiguousarray(
            np.asarray(L0_attn_W_in_init, np.float32).reshape(64, 16 * 64)),
        "woutit": np.ascontiguousarray(
            np.asarray(L0_attn_W_out_init, np.float32).reshape(64, 16 * 64)),
        "w1": np.ascontiguousarray(np.asarray(L0_ffn_W1, np.float32)),
        "b1": np.ascontiguousarray(np.asarray(L0_ffn_b1, np.float32)),
        "w2": np.ascontiguousarray(np.asarray(L0_ffn_W2, np.float32)),
        "b2": np.ascontiguousarray(np.asarray(L0_ffn_b2, np.float32)),
    }
    base.update(host_constants())
    nc = _get_program(L=L)
    in_maps = [dict(base, x=x[i]) for i in range(B)]
    res = run_bass_kernel_spmd(nc, in_maps, core_ids=list(range(B)))
    return np.stack([res.results[i]["out"] for i in range(B)], axis=0)


# version 02
# speedup vs baseline: 1.2237x; 1.2237x over previous
import numpy as np
import ml_dtypes
import concourse.bass as bass
import concourse.mybir as mybir
from concourse.bass_utils import run_bass_kernel_spmd
from concourse.bass_isa_rust import ReduceOp

F32 = mybir.dt.float32
BF16 = mybir.dt.bfloat16
AX = mybir.AxisListType
OP = mybir.AluOpType
ACT = mybir.ActivationFunctionType

D = 1024
NCORE = 8
CH = 64
NP = 8
FF = 4096
SEG = 512
SEGB = 256
BASE_LR = 0.001
BASE_WD = 0.9


class Plan:

    NROT = 8
    ENGS = ("t", "v", "s", "g", "y")
    SEMS = tuple(["t", "v", "s", "g"]
                 + [f"dy{i}" for i in range(8)] + [f"dg{i}" for i in range(8)])

    def __init__(self):
        self.ops = {e: [] for e in self.ENGS}
        self.cnt = {sm: 0 for sm in self.SEMS}
        self.dma_i = {"y": 0, "g": 0}
        self._waited = {e: {sm: 0 for sm in self.SEMS} for e in self.ENGS}

    def op(self, eng, fn, waits=(), inc=True, dma=False):
        w = list(waits)
        if dma:
            i = self.dma_i[eng]
            self.dma_i[eng] += 1
            sem_name = f"d{eng}{i % self.NROT}"
            amt = 16
            if self.cnt[sem_name] > 0:
                w.append((sem_name, self.cnt[sem_name]))
        else:
            sem_name = eng
            amt = 1
        tok = None
        if inc:
            self.cnt[sem_name] += amt
            tok = self.cnt[sem_name]
        ww = []
        for (sm, c) in w:
            if c is None or c <= 0:
                continue
            if self._waited[eng][sm] >= c:
                continue
            self._waited[eng][sm] = c
            ww.append((sm, c))
        self.ops[eng].append((fn, tuple(ww), sem_name if inc else None, amt))
        return tok

    def barrier_tokens(self):
        return dict(self.cnt)


def emit_engine(plan, eng, engobj, sems):
    for fn, waits, inc_sem, amt in plan.ops[eng]:
        for (sm, c) in waits:
            engobj.wait_ge(sems[sm], c)
        inst = fn(engobj)
        if inc_sem is not None:
            assert inst is not None, f"op on {eng} must return instruction to inc"
            inst.then_inc(sems[inc_sem], amt)


def host_constants():
    c = {}
    c["c_ident"] = np.eye(128, dtype=ml_dtypes.bfloat16)
    c["c_identf"] = np.eye(128, dtype=np.float32)
    sel2 = np.zeros((2, 128), dtype=ml_dtypes.bfloat16)
    sel2[0, 0:64] = 1.0
    sel2[1, 64:128] = 1.0
    c["c_sel2"] = sel2
    mask2 = np.zeros((128, 2), dtype=ml_dtypes.bfloat16)
    mask2[0:64, 0] = 1.0
    mask2[64:128, 1] = 1.0
    c["c_mask2"] = mask2
    lsc = np.zeros((64, 1), dtype=np.float32)
    lsc[0:32, 0] = BASE_LR / CH
    lsc[32:64, 0] = BASE_WD / CH
    c["c_lrwdscale"] = lsc
    c["c_two"] = np.full((1, 128), 2.0, dtype=ml_dtypes.bfloat16)
    c["c_pool"] = np.tile(np.array([[1e-5, -0.5, 0.5, 1.0 / (FF * D), 1e-8]],
                                   dtype=np.float32), (128, 1))
    selbig = np.zeros((64, 32, 128), dtype=ml_dtypes.bfloat16)
    for p in range(NP):
        for q in range(4):
            idx = q * 8 + p
            selbig[q * 16 + 2 * p, idx, 0:64] = 1.0
            selbig[q * 16 + 2 * p + 1, idx, 64:128] = 1.0
    c["c_selbig"] = selbig.reshape(64, 32 * 128)
    return c


def build_program(L=4096, dbg=(), skip_passB=False):
    NSEG = L // SEG
    NCH = SEG // CH
    nc = bass.Bass("TRN2", target_bir_lowering=False)

    dr = {}
    dr["x"] = nc.dram_tensor("x", [L, D], F32, kind="ExternalInput").ap()
    for nm, shp in [("wq", [D, D]), ("wk", [D, D]), ("wv", [D, D]),
                    ("wo", [D, D]), ("wlr", [32, D]), ("wbeta", [32, D]),
                    ("winit", [64, 16 * 64]), ("woutit", [64, 16 * 64]),
                    ("w1", [FF, D]), ("b1", [FF]), ("w2", [D, FF]), ("b2", [D])]:
        dr[nm] = nc.dram_tensor(nm, shp, F32, kind="ExternalInput").ap()
    consts = host_constants()
    for nm, arr in consts.items():
        dt = BF16 if arr.dtype == ml_dtypes.bfloat16 else F32
        dr[nm] = nc.dram_tensor(nm, list(arr.shape), dt, kind="ExternalInput").ap()
    dr["out"] = nc.dram_tensor("out", [L, D], F32, kind="ExternalOutput").ap()
    dr["s_sp"] = nc.dram_tensor("s_sp", [L, D], BF16, kind="Internal").ap()
    dr["xn_sp"] = nc.dram_tensor("xn_sp", [8, 128, L], BF16, kind="Internal").ap()
    dbg_dr = {}
    for nm, shp, dt in dbg:
        dbg_dr[nm] = nc.dram_tensor(nm, shp, dt, kind="ExternalOutput").ap()

    plan = Plan()

    import contextlib
    with contextlib.ExitStack() as st:
        def T(*args, **kw):
            return st.enter_context(nc.sbuf_tensor(*args, **kw))

        def PS(*args, **kw):
            return st.enter_context(nc.psum_tensor(*args, **kw))

        ident = T("ident", [128, 128], BF16)
        identf = T("identf", [128, 128], F32)
        sel2 = T("sel2", [2, 128], BF16)
        mask2 = T("mask2", [128, 2], BF16)
        lrwdscale = T("lrwdscale", [64, 1], F32)
        two_row = T("two_row", [1, 128], BF16)
        cpool = T("cpool", [128, 8], F32)
        nc.const_aps.register(F32, 1e-5, cpool[:, 0:1])
        nc.const_aps.register(F32, -0.5, cpool[:, 1:2])
        nc.const_aps.register(F32, 0.5, cpool[:, 2:3])
        nc.const_aps.register(F32, 1.0 / (FF * D), cpool[:, 3:4])
        nc.const_aps.register(F32, 1e-8, cpool[:, 4:5])
        xt = T("xt", [128, 4, D], F32)
        outt = T("outt", [128, 2, D], F32)
        abf = T("abf", [128, 76 * 1024], BF16)
        af32 = T("af32", [128, 5 * 1024], F32)
        pb = [PS("pb0", [128, 512], F32), PS("pb1", [128, 512], F32),
              PS("pb2", [128, 512], F32), PS("pb3", [128, 512], F32),
              PS("pb4", [128, 512], F32), PS("pb5", [128, 1024], BF16),
              PS("pb6", [128, 1024], BF16), PS("pb7", [128, 512], F32)]
        sems = {nm: st.enter_context(nc.semaphore("sem_" + nm))
                for nm in Plan.SEMS}
        block = st.enter_context(nc.Block())
        tb = dict(ident=ident, identf=identf, sel2=sel2, mask2=mask2,
                  lrwdscale=lrwdscale, two_row=two_row, cpool=cpool,
                  xt=xt, outt=outt, abf=abf, af32=af32, pb=pb)

        emit_all(nc, plan, dr, dbg_dr, tb, L, NSEG, NCH, skip_passB)

        @block.tensor
        def _(t):
            emit_engine(plan, "t", t, sems)

        @block.vector
        def _(v):
            emit_engine(plan, "v", v, sems)

        @block.scalar
        def _(s):
            emit_engine(plan, "s", s, sems)

        @block.gpsimd
        def _(g):
            emit_engine(plan, "g", g, sems)

        @block.sync
        def _(y):
            emit_engine(plan, "y", y, sems)

    return nc


class Arena:

    def __init__(self, tensor, n_elems):
        self.t = tensor
        self.n = n_elems
        self.off = 0
        self.marks = {}
        self.slots = {}

    def alloc(self, name, shape):
        n = int(np.prod(shape[1:]))
        assert self.off + n <= self.n, f"arena overflow at {name}: {self.off}+{n}>{self.n}"
        base = self.off
        self.off += n
        self.slots[name] = (base, shape)
        return self.view(name)

    def view(self, name):
        base, shape = self.slots[name]
        ap = self.t[:shape[0], base:base + int(np.prod(shape[1:]))]
        if len(shape) > 2:
            pat = "p (" + " ".join(f"d{i}" for i in range(1, len(shape))) + ") -> p " + \
                  " ".join(f"d{i}" for i in range(1, len(shape)))
            kw = {f"d{i}": shape[i] for i in range(2, len(shape))}
            ap = ap.rearrange(pat, **kw)
        return ap

    def mark(self, tag):
        self.marks[tag] = self.off

    def reset_to(self, tag):
        self.off = self.marks[tag]



Z0 = ("v", 0)


DEBUG_ARENAS = {}


def emit_all(nc, plan, dr, dbg_dr, tb, L, NSEG, NCH, skip_passB=False):
    def P(eng, fn, waits=(), dma=False):
        if dma:
            sem_name = f"d{eng}{plan.dma_i[eng] % plan.NROT}"
            tok = plan.op(eng, fn, waits=waits, inc=True, dma=True)
            return (sem_name, tok)
        tok = plan.op(eng, fn, waits=waits, inc=True, dma=False)
        return (eng, tok)

    pb = tb["pb"]
    A = Arena(tb["abf"], 76 * 1024)
    A4 = Arena(tb["af32"], 5 * 1024)
    DEBUG_ARENAS["A"] = A
    DEBUG_ARENAS["A4"] = A4
    ident = tb["ident"]
    identf = tb["identf"]

    b2bf = A.alloc("b2bf", [1, 1024])
    A.mark("postperm")
    selbig = A.alloc("selbig", [64, 32, 128])
    wqT = A.alloc("wqT", [128, 8, 1024])
    wkT = A.alloc("wkT", [128, 8, 1024])
    wvT = A.alloc("wvT", [128, 8, 1024])
    woT = A.alloc("woT", [128, 8, 1024])
    wlrbT = A.alloc("wlrbT", [128, 8, 64])
    winT = A.alloc("winT", [128, 8, 64])
    wout = A.alloc("wout", [128, 8, 64])
    woutT = A.alloc("woutT", [128, 8, 64])
    A.mark("postAW")
    wnat = A.alloc("wnat", [128, 8, 1024])
    winit_nat = A.alloc("winit_nat", [64, 1024])
    woutit_nat = A.alloc("woutit_nat", [64, 1024])
    wlrb_nat = A.alloc("wlrb_nat", [64, 1024])
    A.reset_to("postAW")

    b1nat = A4.alloc("b1nat", [32, 128])
    b1cols = A4.alloc("b1cols", [128, 32])
    lrwd_f = A4.alloc("lrwd_f", [64, 512])
    lrwdc = A4.alloc("lrwdc", [64, 8])
    lrwdB = A4.alloc("lrwdB", [128, 4, 8, 8])
    SUMS = A4.alloc("SUMS", [128, 16])
    RECIP = A4.alloc("RECIP", [128, 16])
    CLI = A4.alloc("CLI", [128, 8])
    CLO = A4.alloc("CLO", [128, 8])
    nmq = A4.alloc("nmq", [128, 1])
    nmk = A4.alloc("nmk", [128, 1])
    bnst = A4.alloc("bnst", [128, 12])
    mv = A4.alloc("mv", [128, 2])
    rstd = A4.alloc("rstd", [128, 1])
    lntmp = A4.alloc("lntmp", [128, 1])
    xt = tb["xt"]
    outt = tb["outt"]

    pb5w = pb[5].ap().rearrange("p (a b) -> p a b", b=128)
    pb5v = pb[5].ap().rearrange("p (a b) -> p a b", b=64)
    pb6w = pb[6].ap().rearrange("p (a b) -> p a b", b=128)
    pb6v = pb[6].ap().rearrange("p (a b) -> p a b", b=64)
    pb7v = pb[7].ap().rearrange("p (a b) -> p a b", b=64)

    dstmap = {"c_ident": ident, "c_identf": identf, "c_sel2": tb["sel2"],
              "c_mask2": tb["mask2"], "c_lrwdscale": tb["lrwdscale"],
              "c_two": tb["two_row"]}
    cy_toks = []
    for nm, dst in dstmap.items():
        cy_toks.append(P("y", lambda e, dst=dst, nm=nm: e.dma_start(dst[:], dr[nm]),
                         dma=True))
    cy_toks.append(P("y", lambda e: e.dma_start(tb["cpool"][:, 0:5], dr["c_pool"]),
                    dma=True))
    cy_toks.append(P("y", lambda e: e.dma_start(selbig.rearrange("p a b -> p (a b)"),
                                                dr["c_selbig"]), dma=True))
    cy_toks.append(P("y", lambda e: e.dma_start(b1nat[:],
                                                dr["b1"].rearrange("(a b) -> a b", b=128)),
                    dma=True))
    cg_toks2 = [
        P("g", lambda e: e.dma_start(winit_nat[:], dr["winit"]), dma=True),
        P("g", lambda e: e.dma_start(woutit_nat[:], dr["woutit"]), dma=True),
        P("g", lambda e: e.dma_start(wlrb_nat[0:32, :], dr["wlr"]), dma=True),
        P("g", lambda e: e.dma_start(wlrb_nat[32:64, :], dr["wbeta"]), dma=True),
    ]
    cg_toks = [P("g", lambda e: e.dma_start(b2bf[:],
                                            dr["b2"].rearrange("(a b) -> a b", a=1)),
                dma=True)]

    t_tok = P("t", lambda e: e.transpose(pb[7][0:128, 0:32], b1nat[:],
                                         identf[0:32, 0:32]),
              waits=cy_toks)
    tok_b1 = P("v", lambda e: e.tensor_copy(b1cols[:], pb[7][0:128, 0:32]),
               waits=[t_tok])

    for h in range(16):
        p_, hh = h // 2, h % 2
        sl = slice(h * 64, (h + 1) * 64)
        hsl = slice(hh * 64, (hh + 1) * 64)
        t_tok = P("t", (lambda e, sl=sl, p_=p_, hh=hh, hsl=hsl:
                        e.transpose(pb5v[hsl, p_, :], winit_nat[:, sl],
                                    ident[0:64, 0:64], tile_position=(0, 64 * hh))),
                  waits=cg_toks + cg_toks2 + cy_toks + [tok_b1])
        t_tok = P("t", (lambda e, sl=sl, p_=p_, hh=hh, hsl=hsl:
                        e.transpose(pb5v[hsl, 8 + p_, :], woutit_nat[:, sl],
                                    ident[0:64, 0:64], tile_position=(0, 64 * hh))))
        t_tok = P("t", (lambda e, sl=sl, p_=p_, hh=hh, hsl=hsl:
                        e.matmul(pb7v[hsl, p_, :], ident[0:64, 0:64],
                                 woutit_nat[:, sl], start=True, stop=True,
                                 skip_group_check=True, tile_position=(0, 64 * hh))))
    P("v", lambda e: e.tensor_copy(winT[:], pb5v[:, 0:8, :]), waits=[t_tok])
    P("v", lambda e: e.tensor_copy(woutT[:], pb5v[:, 8:16, :]))
    tok_state0 = P("v", lambda e: e.tensor_copy(wout[:], pb7v[:, 0:8, :]))

    for fc in range(8):
        t_tok = P("t", (lambda e, fc=fc:
                        e.transpose(pb6v[0:128, 8 + fc, :],
                                    wlrb_nat[:, fc * 128:(fc + 1) * 128],
                                    ident[0:64, 0:64])),
                  waits=[tok_state0])
    tok_evac = P("v", lambda e: e.tensor_copy(wlrbT[:], pb6v[:, 8:16, :]),
                 waits=[t_tok])

    for wname, wT in (("wq", wqT), ("wk", wkT), ("wv", wvT), ("wo", woT)):
        wnat_free = ("t", plan.cnt["t"])
        tok_wload = P("g", (lambda e, wname=wname:
                            e.dma_start(wnat[:],
                                        dr[wname].rearrange("(c p) f -> p c f", p=128))),
                      waits=[wnat_free], dma=True)
        for fc in range(8):
            for c in range(8):
                t_tok = P("t", (lambda e, fc=fc, c=c:
                                e.transpose(pb5w[:, c, :],
                                            wnat[:, c, fc * 128:(fc + 1) * 128],
                                            ident[:])),
                          waits=[tok_wload, tok_evac])
            tok_evac = P("v", (lambda e, wT=wT, fc=fc:
                               e.tensor_copy(wT[:, fc, :],
                                             pb5w.rearrange("p a b -> p (a b)"))),
                         waits=[t_tok])

    qkv = A.alloc("qkv", [128, 8, NCH, 192])
    xn_seg = A.alloc("xn_seg", [128, 8, SEG])
    o_seg = A.alloc("o_seg", [128, 8, SEG])
    s_seg = A.alloc("s_seg", [128, 4, 1024])
    Eq = A.alloc("Eq", [128, 8, 64])
    Ek = A.alloc("Ek", [128, 8, 64])
    p_q = A.alloc("p_q", [128, 8, 64])
    p_li = A.alloc("p_li", [128, 8, 64])
    p_lo = A.alloc("p_lo", [128, 8, 64])
    m_sb = A.alloc("m_sb", [128, 8, 64])
    a_sb = A.alloc("a_sb", [128, 8, 64])
    t_sb = A.alloc("t_sb", [128, 8, 64])
    gz_sb = A.alloc("gz_sb", [128, 8, 64])
    s2_sb = A.alloc("s2_sb", [2, 8, 64])
    rk2 = A.alloc("rk2", [128, 8, 2])
    kT_sb = A.alloc("kT_sb", [64, 8, 128])
    vT_sb = A.alloc("vT_sb", [64, 8, 128])
    pT_sb = A.alloc("pT_sb", [64, 8, 128])
    gzT_sb = A.alloc("gzT_sb", [64, 8, 128])
    xn_tok = A.alloc("xn_tok", [128, 2, 1024])
    lrwdc_bf = A.alloc("lrwdc_bf", [64, 8])
    A.mark("passA_end")

    pbv = [pb[i].ap().rearrange("p (a b) -> p a b", b=64) for i in range(5)]
    views = dict(qkv=qkv, xn_seg=xn_seg, o_seg=o_seg, s_seg=s_seg, Eq=Eq, Ek=Ek,
                 p_q=p_q, p_li=p_li, p_lo=p_lo, m_sb=m_sb, a_sb=a_sb, t_sb=t_sb,
                 gz_sb=gz_sb, s2_sb=s2_sb, rk2=rk2, kT_sb=kT_sb, vT_sb=vT_sb,
                 pT_sb=pT_sb, gzT_sb=gzT_sb, winT=winT, wout=wout, woutT=woutT,
                 SUMS=SUMS, RECIP=RECIP, CLI=CLI, CLO=CLO, nmq=nmq, nmk=nmk,
                 lrwdB=lrwdB, pbv=pbv, pb5w=pb5w, pb6w=pb6w,
                 wqT=wqT, wkT=wkT, wvT=wvT, woT=woT, wlrbT=wlrbT,
                 lrwd_f=lrwd_f, lrwdc=lrwdc, bnst=bnst, mv=mv, rstd=rstd,
                 lntmp=lntmp, xn_tok=xn_tok, selbig=selbig,
                 lrwdc_bf=lrwdc_bf)

    wstream = A4.alloc("wstream", [128, 2, 1024])
    acc1 = A4.alloc("accw1", [128, 1])
    acc2 = A4.alloc("accw2", [128, 1])
    tabs = A4.alloc("tabsw", [128, 1])
    t1bufs = dict(wstream=wstream, acc1=acc1, acc2=acc2, tabs=tabs)

    ck = {
        "t1_i": 0, "t1_ws": [Z0, Z0], "t1_tabs": Z0, "t1_acc": Z0,
        "upd_win": tok_state0, "upd_wout": tok_state0, "wt_evac": tok_state0,
        "b0": Z0, "b1": Z0, "b2": Z0, "b3": Z0, "b4": Z0,
        "b5": tok_evac, "b6": tok_evac, "b7": tok_b1,
        "eq": Z0, "ek": Z0, "pq": Z0, "pli": Z0, "plo": Z0, "m": Z0, "a": Z0,
        "gz": Z0, "rk2": Z0, "s2": Z0, "kt": Z0, "vt": Z0, "pt": Z0, "gzt": Z0,
        "nm": Z0, "qkv": Z0, "xnseg": Z0, "xnsp_dma": Z0, "oseg": Z0,
        "sseg": Z0, "lrwdf": Z0,
        "lrwdB_w": Z0, "xt": [Z0, Z0, Z0, Z0], "xntok": [Z0, Z0], "mv": Z0,
        "sdma": Z0,
        "o_evac": ("s", 0), "qkv_ready": ("s", 0), "lrwdB_ready": ("v", 0),
    }

    views.update(t1bufs)
    for seg in range(NSEG):
        emit_segment(plan, P, dr, tb, views, ck, seg, NSEG, NCH)
        emit_t1_tiles(plan, P, dr, views, ck, 64 // NSEG)

    barA = plan.barrier_tokens()
    bar_toks = [(sm, c) for sm, c in barA.items() if c > 0]
    if not skip_passB:
        A.reset_to("postperm")
        emit_passB(nc, plan, P, dr, dbg_dr, tb, A, A4, bar_toks, L,
                   b2bf, b1cols, ident, identf, ck, t1bufs)

    fin = [(sm, plan.cnt[sm]) for sm in Plan.SEMS
           if sm.startswith("d") and plan.cnt[sm] > 0]
    plan.op("y", lambda e: None, waits=fin, inc=False)


def emit_segment(plan, P, dr, tb, V, ck, seg, NSEG, NCH):
    pb = tb["pb"]
    pbv = V["pbv"]
    pb5w, pb6w = V["pb5w"], V["pb6w"]
    ident = tb["ident"]
    xt = tb["xt"]
    qkv, xn_seg, o_seg, s_seg = V["qkv"], V["xn_seg"], V["o_seg"], V["s_seg"]
    A4v = {k: V[k] for k in ("SUMS", "RECIP", "CLI", "CLO", "nmq", "nmk", "lrwdB")}

    for tsub in range(4):
        gidx = seg * 4 + tsub
        par = tsub % 2
        tok_x = P("y", (lambda e, gidx=gidx, par=par:
                        e.dma_start(xt[:, par, :],
                                    dr["x"][gidx * 128:(gidx + 1) * 128, :])),
                  waits=[ck["xt"][par]], dma=True)
        tok_bn = P("v", (lambda e, par=par:
                         e.bn_stats(V["bnst"][:, 0:6], xt[:, par, 0:512])),
                   waits=[tok_x, ck["mv"]])
        tok_bn = P("v", (lambda e, par=par:
                         e.bn_stats(V["bnst"][:, 6:12], xt[:, par, 512:1024])))
        tok_mv = P("v", lambda e: e.bn_aggr(V["mv"][:], V["bnst"][:, 0:12]),
                   waits=[tok_bn])
        tok_ln = P("s", lambda e: e.activation(V["lntmp"][:], V["mv"][:, 1:2],
                                               ACT.Ln, bias=1e-5),
                   waits=[tok_mv])
        tok_rstd = P("s", lambda e: e.activation(V["rstd"][:], V["lntmp"][:],
                                                 ACT.Exp, scale=-0.5),
                     waits=[tok_ln])
        ck["mv"] = tok_rstd
        tok_xntok = P("v", (lambda e, par=par:
                            e.tensor_scalar(V["xn_tok"][:, par, :], xt[:, par, :],
                                            V["mv"][:, 0:1], V["rstd"][:],
                                            op0=OP.subtract, op1=OP.mult)),
                      waits=[tok_rstd, tok_mv, ck["xntok"][par]])
        ck["xt"][par] = tok_xntok
        for fc in range(8):
            t_tok = P("t", (lambda e, par=par, fc=fc:
                            e.transpose(pb5w[:, fc, :],
                                        V["xn_tok"][:, par, fc * 128:(fc + 1) * 128],
                                        ident[:])),
                      waits=[tok_xntok, ck["b5"]])
        ck["xntok"][par] = t_tok
        tok_ev = P("v", (lambda e, tsub=tsub:
                         e.tensor_copy(xn_seg[:, :, tsub * 128:(tsub + 1) * 128],
                                       pb5w[:, :, :])),
                   waits=[t_tok, ck["xnseg"], ck["xnsp_dma"]])
        ck["b5"] = tok_ev
    tok_xn = tok_ev

    sev = [("s", 0), ("s", 0)]
    bank_free = [ck["b0"], ck["b1"]]
    pp = 0
    for wTname, coff, actf in (("wqT", 0, ACT.Silu), ("wkT", 64, ACT.Silu),
                               ("wvT", 128, ACT.Copy)):
        wT = V[wTname]
        for m in range(8):
            bank = pb[pp % 2]
            for fc in range(8):
                t_tok = P("t", (lambda e, wT=wT, m=m, fc=fc, bank=bank:
                                e.matmul(bank[:, :],
                                         wT[:, fc, m * 128:(m + 1) * 128],
                                         xn_seg[:, fc, :],
                                         start=(fc == 0), stop=(fc == 7),
                                         skip_group_check=True)),
                          waits=[tok_xn, sev[pp % 2], bank_free[pp % 2]])
            bview = bank.ap().rearrange("p (a b) -> p a b", b=64)
            tok_evac = P("s", (lambda e, bview=bview, m=m, coff=coff, actf=actf:
                               e.activation(qkv[:, m, :, coff:coff + 64],
                                            bview[:, :, :], actf)),
                         waits=[t_tok, ck["qkv"]])
            sev[pp % 2] = tok_evac
            pp += 1
    ck["b0"] = sev[0]
    ck["b1"] = sev[1]
    ck["qkv_ready"] = tok_evac
    for fc in range(8):
        t_tok = P("t", (lambda e, fc=fc:
                        e.matmul(pb[2][0:64, :], V["wlrbT"][:, fc, :],
                                 xn_seg[:, fc, :], start=(fc == 0),
                                 stop=(fc == 7), skip_group_check=True)),
                  waits=[tok_xn, ck["b2"]])
    tok_sg = P("s", lambda e: e.activation(V["lrwd_f"][:], pb[2][0:64, :],
                                           ACT.Sigmoid),
               waits=[t_tok, ck["lrwdf"]])
    ck["b2"] = tok_sg
    tok_rd = P("v", (lambda e:
                     e.reduce_sum(V["lrwdc"][:],
                                  V["lrwd_f"].rearrange("p (a b) -> p a b", b=64),
                                  axis=AX.X)),
              waits=[tok_sg])
    ck["lrwdf"] = tok_rd
    tok_lc = P("v", lambda e: e.tensor_scalar(V["lrwdc_bf"][:], V["lrwdc"][:],
                                              tb["lrwdscale"][:], None,
                                              op0=OP.mult), waits=[tok_rd])
    pb3b = pb[3].ap().rearrange("p (a b) -> p a b", b=8)
    for idx in range(32):
        t_tok = P("t", (lambda e, idx=idx:
                        e.matmul(pb3b[:, idx, :],
                                 V["selbig"][:, idx, :],
                                 V["lrwdc_bf"][:], start=True, stop=True,
                                 skip_group_check=True)),
                  waits=[tok_lc, ck["b3"], ck["lrwdB_w"]])
    tok_lB = P("v", (lambda e:
                     e.tensor_copy(V["lrwdB"].rearrange("p a b c -> p (a b c)"),
                                   pb[3][:, 0:256])),
               waits=[t_tok])
    ck["b3"] = tok_lB
    ck["lrwdB_ready"] = tok_lB
    ck["xnseg"] = ("t", plan.cnt["t"])
    tok_xnsp = P("g", (lambda e, seg=seg:
                       e.dma_start(dr["xn_sp"][:, :, seg * SEG:(seg + 1) * SEG]
                                   .rearrange("a p t -> p a t"),
                                   V["xn_seg"][:])),
                 waits=[tok_xn], dma=True)
    ck["xnsp_dma"] = tok_xnsp

    for ci in range(NCH):
        emit_scan_chunk(plan, P, tb, V, ck, ci)
    ck["qkv"] = ("t", plan.cnt["t"])

    tok_sev = ("s", 0)
    for tsub in range(4):
        for half in range(2):
            bank = pb[2 + half]
            bfree = ck["b2"] if half == 0 else ck["b3"]
            for fc in range(8):
                t_tok = P("t", (lambda e, tsub=tsub, half=half, fc=fc, bank=bank:
                                e.matmul(bank[:, :],
                                         o_seg[:, fc, tsub * 128:(tsub + 1) * 128],
                                         V["woT"][:, fc, half * 512:(half + 1) * 512],
                                         start=(fc == 0), stop=(fc == 7),
                                         skip_group_check=True)),
                          waits=[ck["o_evac"], bfree])
            tok_sev = P("s", (lambda e, tsub=tsub, half=half, bank=bank:
                              e.activation(s_seg[:, tsub, half * 512:(half + 1) * 512],
                                           bank[:, :], ACT.Copy)),
                        waits=[t_tok, ck["sseg"]])
            if half == 0:
                ck["b2"] = tok_sev
            else:
                ck["b3"] = tok_sev
    ck["oseg"] = t_tok
    tok_sdma = P("g", (lambda e, seg=seg:
                       e.dma_start(dr["s_sp"][seg * SEG:(seg + 1) * SEG, :]
                                   .rearrange("(a p) f -> p a f", p=128),
                                   s_seg[:])),
                 waits=[tok_sev, ck["sdma"]], dma=True)
    ck["sseg"] = tok_sdma
    ck["sdma"] = tok_sdma


def emit_scan_chunk(plan, P, tb, V, ck, ci):
    pb = tb["pb"]
    pbv = V["pbv"]
    pb5w, pb6w = V["pb5w"], V["pb6w"]
    ident = tb["ident"]
    qkv, winT, wout, woutT = V["qkv"], V["winT"], V["wout"], V["woutT"]
    Eq, Ek, p_q, p_li, p_lo = V["Eq"], V["Ek"], V["p_q"], V["p_li"], V["p_lo"]
    m_sb, a_sb, t_sb, gz_sb = V["m_sb"], V["a_sb"], V["t_sb"], V["gz_sb"]
    s2_sb, rk2 = V["s2_sb"], V["rk2"]
    kT_sb, vT_sb, pT_sb, gzT_sb = V["kT_sb"], V["vT_sb"], V["pT_sb"], V["gzT_sb"]
    SUMS, RECIP, CLI, CLO = V["SUMS"], V["RECIP"], V["CLI"], V["CLO"]
    nmq, nmk, lrwdB = V["nmq"], V["nmk"], V["lrwdB"]
    pb5WT = pb[5].ap().rearrange("p (a b) -> p a b", b=64)

    def heads():
        for p in range(NP):
            for hh in range(2):
                yield p, hh, slice(hh * 64, (hh + 1) * 64), (0 if hh == 0 else 64)

    for p, hh, hsl, off in heads():
        t_tok = P("t", (lambda e, p=p, hh=hh, hsl=hsl, off=off:
                        e.matmul(pbv[0][hsl, p, :], winT[hsl, p, :],
                                 qkv[hsl, p, ci, 0:64], start=True, stop=True,
                                 skip_group_check=True, tile_position=(off, off))),
                  waits=[ck["upd_win"], ck["qkv_ready"], ck["b0"], ck["nm"]])
        t_tok = P("t", (lambda e, p=p, hh=hh, hsl=hsl, off=off:
                        e.matmul(pbv[1][hsl, p, :], winT[hsl, p, :],
                                 qkv[hsl, p, ci, 64:128], start=True, stop=True,
                                 skip_group_check=True, tile_position=(off, off))),
                  waits=[ck["b1"]])
    tok_z = t_tok

    for p in range(NP):
        t_tok = P("t", (lambda e, p=p:
                        e.transpose(pb5w[0:64, p, :], qkv[:, p, ci, 64:128],
                                    ident[:])),
                  waits=[ck["b5"]])
    tok_trk = t_tok
    for p in range(NP):
        t_tok = P("t", (lambda e, p=p:
                        e.transpose(pb6w[0:64, p, :], qkv[:, p, ci, 128:192],
                                    ident[:])),
                  waits=[ck["b6"]])
    tok_trv = t_tok
    tok_kt = P("v", lambda e: e.tensor_copy(kT_sb[:], pb5w[0:64, 0:8, :]),
               waits=[tok_trk, ck["kt"]])
    ck["b5"] = tok_kt
    tok_vt = P("s", lambda e: e.activation(vT_sb[:], pb6w[0:64, 0:8, :], ACT.Copy),
               waits=[tok_trv, ck["vt"]])
    ck["b6"] = tok_vt

    for p, hh, hsl, off in heads():
        t_tok = P("t", (lambda e, p=p, hh=hh, hsl=hsl, off=off:
                        e.matmul(pbv[2][hsl, p, :], woutT[hsl, p, :],
                                 qkv[hsl, p, ci, 128:192], start=True, stop=True,
                                 skip_group_check=True, tile_position=(off, off))),
                  waits=[ck["wt_evac"], ck["b2"]])
    tok_amm = t_tok

    tok_nmq = P("v", lambda e: e.reduce_max(nmq[:], pb[0][:, :], axis=AX.X,
                                            negate=True),
                waits=[tok_z, ck["nm"]])
    tok_nmk = P("v", lambda e: e.reduce_max(nmk[:], pb[1][:, :], axis=AX.X,
                                            negate=True))
    tok_eq = P("s", lambda e: e.activation(Eq[:], pb[0][:, :], ACT.Exp,
                                           bias=nmq[:]),
               waits=[tok_nmq, ck["eq"]])
    tok_ek = P("s", lambda e: e.activation(Ek[:], pb[1][:, :], ACT.Exp,
                                           bias=nmk[:]),
               waits=[tok_nmk, ck["ek"]])
    ck["b0"] = tok_eq
    ck["b1"] = tok_ek
    ck["nm"] = tok_ek
    tok_sums = P("v", lambda e: e.reduce_sum(SUMS[:, 0:8], Eq[:], axis=AX.X),
                 waits=[tok_eq])
    tok_sums = P("v", lambda e: e.reduce_sum(SUMS[:, 8:16], Ek[:], axis=AX.X),
                 waits=[tok_ek])
    tok_rec = P("v", lambda e: e.reciprocal(RECIP[:], SUMS[:]),
                waits=[tok_sums])
    tok_cli = P("v", (lambda e:
                      e.tensor_tensor(CLI[:], RECIP[:, 8:16],
                                      lrwdB[:, 0, :, ci], OP.mult)),
                waits=[ck["lrwdB_ready"], tok_rec])
    tok_clo = P("v", (lambda e:
                      e.tensor_tensor(CLO[:], RECIP[:, 8:16],
                                      lrwdB[:, 1, :, ci], OP.mult)))
    for p in range(NP):
        tok_pq = P("v", (lambda e, p=p:
                         e.tensor_scalar(p_q[:, p, :], Eq[:, p, :],
                                         RECIP[:, p:p + 1], None, op0=OP.mult)),
                   waits=[ck["pq"], tok_rec])
        tok_pli = P("v", (lambda e, p=p:
                          e.tensor_scalar(p_li[:, p, :], Ek[:, p, :],
                                          CLI[:, p:p + 1], None, op0=OP.mult)),
                    waits=[ck["pli"], tok_clo])
        tok_plo = P("v", (lambda e, p=p:
                          e.tensor_scalar(p_lo[:, p, :], Ek[:, p, :],
                                          CLO[:, p:p + 1], None, op0=OP.mult)),
                    waits=[ck["plo"]])
        tok_rk2 = P("v", (lambda e, p=p:
                          e.tensor_scalar(rk2[:, p, :], tb["mask2"][:],
                                          RECIP[:, 8 + p:9 + p], None,
                                          op0=OP.mult)),
                    waits=[ck["rk2"]])
    tok_eqfree = tok_rk2

    for p, hh, hsl, off in heads():
        t_tok = P("t", (lambda e, p=p, hh=hh, hsl=hsl, off=off:
                        e.matmul(pbv[3][hsl, p, :], wout[hsl, p, :],
                                 p_q[hsl, p, :], start=True, stop=True,
                                 skip_group_check=True, tile_position=(off, off))),
                  waits=[ck["upd_wout"], ck["b3"], tok_pq])
    tok_omm = t_tok
    ck["pq"] = tok_omm
    tok_oev = P("s", (lambda e, ci=ci:
                      e.activation(V["o_seg"][:, :, ci * 64:(ci + 1) * 64],
                                   pbv[3][:, 0:8, :], ACT.Copy)),
                waits=[tok_omm, ck["oseg"]])
    ck["b3"] = tok_oev
    ck["o_evac"] = tok_oev

    tok_aev = P("s", lambda e: e.activation(a_sb[:], pbv[2][:, 0:8, :], ACT.Copy),
                waits=[tok_amm, ck["a"]])
    tok_m = P("v", lambda e: e.tensor_tensor(m_sb[:], Ek[:], pbv[2][:, 0:8, :],
                                             OP.mult),
              waits=[tok_ek, tok_aev, ck["m"]])
    ck["b2"] = tok_m
    ck["eq"] = tok_eqfree
    ck["ek"] = tok_m
    for p in range(NP):
        t_tok = P("t", (lambda e, p=p:
                        e.matmul(pbv[4][0:2, p, :], rk2[:, p, :], m_sb[:, p, :],
                                 start=True, stop=True, skip_group_check=True)),
                  waits=[tok_m, tok_rk2, ck["b4"]])
    tok_s2mm = t_tok
    ck["rk2"] = tok_s2mm
    ck["m"] = tok_s2mm
    tok_s2ev = P("s", lambda e: e.activation(s2_sb[:], pbv[4][0:2, 0:8, :],
                                             ACT.Copy),
                 waits=[tok_s2mm, ck["s2"]])
    tok_sb = P("t", (lambda e:
                     e.matmul(pb[4][:, :], tb["sel2"][:],
                              s2_sb.rearrange("p a b -> p (a b)"),
                              start=True, stop=True, skip_group_check=True)),
              waits=[tok_s2ev])
    ck["s2"] = tok_sb
    tok_t = P("v", lambda e: e.tensor_tensor(t_sb[:], pb[4].ap()
                                             .rearrange("p (a b) -> p a b", b=64),
                                             a_sb[:], OP.subtract),
              waits=[tok_sb, tok_aev])
    ck["b4"] = tok_t
    ck["a"] = tok_t
    tok_gz = P("v", lambda e: e.tensor_tensor(gz_sb[:], t_sb[:], p_li[:], OP.mult),
               waits=[tok_pli, tok_t, ck["gz"]])
    ck["pli"] = tok_gz

    for p in range(NP):
        t_tok = P("t", (lambda e, p=p:
                        e.transpose(pb5w[0:64, p, :], p_lo[:, p, :], ident[:])),
                  waits=[tok_plo, ck["b5"]])
    tok_trp = t_tok
    tok_pt = P("v", lambda e: e.tensor_copy(pT_sb[:], pb5w[0:64, 0:8, :]),
               waits=[tok_trp, ck["pt"]])
    ck["b5"] = tok_pt
    ck["plo"] = tok_trp
    for p in range(NP):
        t_tok = P("t", (lambda e, p=p:
                        e.transpose(pb6w[0:64, p, :], gz_sb[:, p, :], ident[:])),
                  waits=[tok_gz, ck["b6"]])
    tok_trgz = t_tok
    ck["gz"] = tok_trgz
    tok_gzt = P("s", lambda e: e.activation(gzT_sb[:], pb6w[0:64, 0:8, :],
                                            ACT.Copy),
                waits=[tok_trgz, ck["gzt"]])
    ck["b6"] = tok_gzt

    for p, hh, hsl, off in heads():
        t_tok = P("t", (lambda e, p=p, hh=hh, hsl=hsl, off=off:
                        e.matmul(pbv[0][hsl, p, :],
                                 pT_sb[0:64, p, hh * 64:(hh + 1) * 64],
                                 vT_sb[0:64, p, hh * 64:(hh + 1) * 64],
                                 start=True, stop=True, skip_group_check=True,
                                 tile_position=(0, off))),
                  waits=[tok_pt, tok_vt, ck["b0"]])
    tok_g11 = t_tok
    for p, hh, hsl, off in heads():
        t_tok = P("t", (lambda e, p=p, hh=hh, hsl=hsl, off=off:
                        e.matmul(pbv[1][hsl, p, :],
                                 kT_sb[0:64, p, hh * 64:(hh + 1) * 64],
                                 gzT_sb[0:64, p, hh * 64:(hh + 1) * 64],
                                 start=True, stop=True, skip_group_check=True,
                                 tile_position=(0, off))),
                  waits=[tok_kt, tok_gzt, ck["b1"]])
    tok_g13 = t_tok
    ck["kt"] = tok_g13
    ck["vt"] = tok_g11
    ck["pt"] = tok_g11
    ck["gzt"] = tok_g13

    for p in range(NP):
        tok_uo = P("v", (lambda e, p=p:
                         e.scalar_tensor_tensor(wout[:, p, :], wout[:, p, :],
                                                lrwdB[:, 3, p, ci:ci + 1],
                                                pbv[0][:, p, :], op0=OP.mult,
                                                op1=OP.add)),
                   waits=[tok_g11, tok_omm, ck["upd_wout"], ck["lrwdB_ready"]])
        tok_ui = P("v", (lambda e, p=p:
                         e.scalar_tensor_tensor(winT[:, p, :], winT[:, p, :],
                                                lrwdB[:, 2, p, ci:ci + 1],
                                                pbv[1][:, p, :], op0=OP.mult,
                                                op1=OP.subtract)),
                   waits=[tok_g13, tok_z, ck["upd_win"]])
    ck["upd_win"] = tok_ui
    ck["upd_wout"] = tok_uo
    ck["b0"] = tok_uo
    ck["b1"] = tok_ui

    for p, hh, hsl, off in heads():
        t_tok = P("t", (lambda e, p=p, hh=hh, hsl=hsl, off=off:
                        e.transpose(pb5WT[hsl, p, :], wout[hsl, p, :],
                                    ident[hsl, hsl], tile_position=(off, off))),
                  waits=[tok_uo, ck["b5"], tok_amm])
    tok_trw = t_tok
    tok_wt = P("s", lambda e: e.activation(woutT[:], pb5WT[:, 0:8, :], ACT.Copy),
               waits=[tok_trw])
    ck["b5"] = tok_wt
    ck["wt_evac"] = tok_wt


def w2_tile_src_g(dr, c):
    rt, cc = c // 4, c % 4
    return dr["w2"][rt * 128:(rt + 1) * 128, cc * 1024:(cc + 1) * 1024]


def emit_t1_tiles(plan, P, dr, V, ck, n):
    wstream, tabs = V["wstream"], V["tabs"]
    for _ in range(n):
        i = ck["t1_i"]
        if i >= 64:
            return
        ck["t1_i"] += 1
        par = i % 2
        if i < 32:
            srcap = dr["w1"][i * 128:(i + 1) * 128, :]
            acc = V["acc1"]
        else:
            srcap = w2_tile_src_g(dr, i - 32)
            acc = V["acc2"]
        tok_l = P("y", (lambda e, srcap=srcap, par=par:
                        e.dma_start(wstream[:, par, :], srcap)),
                  waits=[ck["t1_ws"][par]], dma=True)
        tok_tb = P("v", (lambda e, par=par:
                         e.reduce_sum(tabs[:], wstream[:, par, :], axis=AX.X,
                                      apply_absolute_value=True)),
                   waits=[tok_l, ck["t1_tabs"]])
        if i % 32 == 0:
            tok_a = P("v", (lambda e, acc=acc: e.tensor_copy(acc[:], tabs[:])),
                      waits=[tok_tb])
        else:
            tok_a = P("v", (lambda e, acc=acc:
                            e.tensor_tensor(acc[:], acc[:], tabs[:], OP.add)),
                      waits=[tok_tb])
        ck["t1_ws"][par] = tok_a
        ck["t1_tabs"] = tok_a
    return


def emit_passB(nc, plan, P, dr, dbg_dr, tb, A, A4, bar_toks, L,
               b2bf, b1cols, ident, identf, ck, t1bufs):
    pb = tb["pb"]
    xt = tb["xt"]
    outt = tb["outt"]
    pb5w = pb[5].ap().rearrange("p (a b) -> p a b", b=128)
    pb6w = pb[6].ap().rearrange("p (a b) -> p a b", b=128)
    NBT = L // SEGB
    wstream = t1bufs["wstream"]
    acc1, acc2, tabs = t1bufs["acc1"], t1bufs["acc2"], t1bufs["tabs"]

    for e in Plan.ENGS:
        plan.op(e, lambda _e: None, waits=bar_toks, inc=False)

    w1T = A.alloc("w1T", [128, 8, FF])
    w2T = A.alloc("w2T", [128, 32, 1024])
    A.mark("tern_tmp")
    w12scr = A.alloc("w12scr", [128, 1024])
    sgA = A.alloc("sgA", [128, 1024])
    A.reset_to("tern_tmp")
    hT = A.alloc("hT", [128, 2, SEGB])
    xnB = A.alloc("xnB", [128, 2, 8, SEGB])
    s_rd = A.alloc("s_rd", [128, 2, 1024])
    g_sb = A.alloc("g_sb", [128, 2, 1024])

    tot = A4.alloc("totw", [1, 1])
    inv1 = A4.alloc("invw1", [128, 1])
    inv2 = A4.alloc("invw2", [128, 1])
    ones_col = A4.alloc("ones_col", [128, 1])
    ones_row = A4.alloc("ones_row", [1, 128])


    tok_o1 = P("v", lambda e: e.memset(ones_col[:], 1.0))
    tok_o2 = P("v", lambda e: e.memset(ones_row[:], 1.0))
    tabs_free = ck["t1_tabs"]
    for acc, inv in ((acc1, inv1), (acc2, inv2)):
        t_tok = P("t", (lambda e, acc=acc:
                        e.matmul(pb[7][0:1, 0:1], ones_col[:], acc[:],
                                 start=True, stop=True, skip_group_check=True)),
                  waits=[ck["t1_tabs"], tok_o1])
        tok_cp = P("v", lambda e: e.tensor_copy(tot[0:1, :], pb[7][0:1, 0:1]),
                   waits=[t_tok])
        t_tok = P("t", (lambda e:
                        e.matmul(pb[7][0:128, 1:2], ones_row[:], tot[0:1, :],
                                 start=True, stop=True, skip_group_check=True)),
                  waits=[tok_cp, tok_o2])
        tok_t1 = P("v", (lambda e: e.tensor_scalar(tabs[:], pb[7][0:128, 1:2],
                                                   1.0 / (FF * D), None,
                                                   op0=OP.mult)),
                   waits=[t_tok, tabs_free])
        tok_t2 = P("v", (lambda e: e.tensor_scalar(tabs[:], tabs[:], 1e-8, None,
                                                   op0=OP.max)), waits=[tok_t1])
        tok_i = P("v", (lambda e, inv=inv: e.reciprocal(inv[:], tabs[:])),
                  waits=[tok_t2])
        tabs_free = tok_i

    ws_free = list(ck["t1_ws"])
    scr_free = Z0
    tr_banks = [pb5w, pb6w]
    bank_ev = [Z0, Z0]
    tok_trev = Z0
    for wname, wT, inv in (("w1", w1T, inv1), ("w2", w2T, inv2)):
        for c in range(32):
            par = c % 2
            srcap = (dr["w1"][c * 128:(c + 1) * 128, :] if wname == "w1"
                     else w2_tile_src_g(dr, c))
            tok_l = P("y", (lambda e, srcap=srcap, par=par:
                            e.dma_start(wstream[:, par, :], srcap)),
                      waits=[ws_free[par]], dma=True)
            tok_sa = P("s", (lambda e, par=par, inv=inv:
                             e.activation(w12scr[:], wstream[:, par, :], ACT.Sign,
                                          bias=-0.5, scale=inv[:])),
                       waits=[tok_l, tok_i, scr_free])
            tok_sb_ = P("s", (lambda e, par=par, inv=inv:
                              e.activation(sgA[:], wstream[:, par, :], ACT.Sign,
                                           bias=0.5, scale=inv[:])))
            ws_free[par] = tok_sb_
            tok_w = P("v", lambda e: e.tensor_tensor(w12scr[:], w12scr[:], sgA[:],
                                                     OP.add),
                      waits=[tok_sb_])
            pbx = tr_banks[par]
            for j in range(8):
                t_tok = P("t", (lambda e, j=j, pbx=pbx:
                                e.transpose(pbx[:, j, :],
                                            w12scr[:, j * 128:(j + 1) * 128],
                                            ident[:])),
                          waits=[tok_w, bank_ev[par]])
            scr_free = t_tok
            if wname == "w1":
                dst = w1T[:, :, c * 128:(c + 1) * 128]
            else:
                rt, cc = c // 4, c % 4
                dst = w2T[:, cc * 8:(cc + 1) * 8, rt * 128:(rt + 1) * 128]
            tok_trev = P("v", (lambda e, dst=dst, pbx=pbx:
                               e.tensor_copy(dst, pbx[:, :, :])),
                         waits=[t_tok])
            bank_ev[par] = tok_trev
    tok_tern = tok_trev
    tok_tern_t = ("t", plan.cnt["t"])

    xt_free = [Z0, Z0, Z0, Z0]
    h_free = [Z0, Z0]
    bank_free = {i: Z0 for i in range(4)}
    mm1_free = {0: Z0, 1: tabs_free}
    g_free = [Z0, Z0]
    srd_free = Z0
    out_free = [Z0, Z0]
    xnB_free = [Z0, Z0]
    xnB_tok = [Z0, Z0]
    srd_tok = Z0

    def dma_xnB(bt):
        par = bt % 2
        tok = P("y", (lambda e, bt=bt, par=par:
                      e.dma_start(xnB[:, par, :, :],
                                  dr["xn_sp"][:, :, bt * SEGB:(bt + 1) * SEGB]
                                  .rearrange("a p t -> p a t"))),
                waits=[xnB_free[par], tok_tern], dma=True)
        xnB_tok[par] = tok

    dma_xnB(0)
    for bt in range(NBT):
        par = bt % 2
        if bt + 1 < NBT:
            dma_xnB(bt + 1)
        srd_tok = P("y", (lambda e, bt=bt:
                          e.dma_start(s_rd[:, 0:2, :],
                                      dr["s_sp"][bt * SEGB:(bt + 1) * SEGB, :]
                                      .rearrange("(a p) f -> p a f", p=128))),
                    waits=[srd_free, tok_tern], dma=True)
        for tsub in range(2):
            gidx = bt * 2 + tsub
            slot = gidx % 4
            xt_tok = P("y", (lambda e, gidx=gidx, slot=slot:
                             e.dma_start(xt[:, slot, :],
                                         dr["x"][gidx * 128:(gidx + 1) * 128, :])),
                       waits=[xt_free[slot]], dma=True)
            if tsub == 0:
                xt_tok0 = xt_tok
            else:
                xt_tok1 = xt_tok
        for f1c in range(32):
            hpar = f1c % 2
            bank = pb[4] if hpar == 0 else pb[7]
            for fc in range(8):
                t_tok = P("t", (lambda e, f1c=f1c, fc=fc, bank=bank, par=par:
                                e.matmul(bank[:, 0:SEGB],
                                         w1T[:, fc, f1c * 128:(f1c + 1) * 128],
                                         xnB[:, par, fc, :], start=(fc == 0),
                                         stop=(fc == 7), skip_group_check=True)),
                          waits=[xnB_tok[par], mm1_free[hpar], tok_tern_t])
            tok_mm1 = t_tok
            if f1c > 0:
                pf1c, phpar = f1c - 1, (f1c - 1) % 2
                for tsub in range(2):
                    for half in range(2):
                        j = tsub * 2 + half
                        t_tok = P("t", (lambda e, pf1c=pf1c, phpar=phpar,
                                        tsub=tsub, half=half, j=j:
                                        e.matmul(pb[j][:, :],
                                                 hT[:, phpar, tsub * 128:(tsub + 1) * 128],
                                                 w2T[:, pf1c, half * 512:(half + 1) * 512],
                                                 start=(pf1c == 0), stop=False,
                                                 skip_group_check=True)),
                                  waits=[tok_gelu_prev, bank_free[j]])
                mm1_free[phpar] = t_tok
                h_free[phpar] = t_tok
            tok_gelu_prev = P("s", (lambda e, hpar=hpar, f1c=f1c, bank=bank:
                                    e.activation(hT[:, hpar, :], bank[:, 0:SEGB],
                                                 ACT.Gelu, scale=0.5,
                                                 bias=b1cols[:, f1c:f1c + 1])),
                              waits=[tok_mm1, h_free[hpar], tok_tern])
        pf1c, phpar = 31, 31 % 2
        for tsub in range(2):
            for half in range(2):
                j = tsub * 2 + half
                t_tok = P("t", (lambda e, pf1c=pf1c, phpar=phpar, tsub=tsub,
                                half=half, j=j:
                                e.matmul(pb[j][:, :],
                                         hT[:, phpar, tsub * 128:(tsub + 1) * 128],
                                         w2T[:, pf1c, half * 512:(half + 1) * 512],
                                         start=False, stop=False,
                                         skip_group_check=True)),
                          waits=[tok_gelu_prev, bank_free[j]])
        mm1_free[phpar] = t_tok
        h_free[phpar] = t_tok
        for tsub in range(2):
            for half in range(2):
                j = tsub * 2 + half
                t_tok = P("t", (lambda e, j=j, half=half:
                                e.matmul(pb[j][:, :], tb["two_row"][:],
                                         b2bf[0:1, half * 512:(half + 1) * 512],
                                         start=False, stop=True,
                                         skip_group_check=True)))
                tok_g = P("s", (lambda e, j=j, tsub=tsub, half=half:
                                e.activation(g_sb[:, tsub, half * 512:(half + 1) * 512],
                                             pb[j][:, :], ACT.Silu, scale=0.5)),
                          waits=[t_tok, g_free[tsub]])
                bank_free[j] = tok_g
            gidx = bt * 2 + tsub
            slot = gidx % 4
            opar = gidx % 2
            xt_tok = xt_tok0 if tsub == 0 else xt_tok1
            tok_c1 = P("v", (lambda e, tsub=tsub, opar=opar:
                             e.tensor_tensor(outt[:, opar, :], s_rd[:, tsub, :],
                                             g_sb[:, tsub, :], OP.mult)),
                       waits=[tok_g, srd_tok, out_free[opar]])
            tok_c2 = P("v", (lambda e, slot=slot, opar=opar:
                             e.tensor_tensor(outt[:, opar, :], outt[:, opar, :],
                                             xt[:, slot, :], OP.add)),
                       waits=[tok_c1, xt_tok])
            g_free[tsub] = tok_c2
            xt_free[slot] = tok_c2
            tok_od = P("g", (lambda e, gidx=gidx, opar=opar:
                             e.dma_start(dr["out"][gidx * 128:(gidx + 1) * 128, :],
                                         outt[:, opar, :])),
                       waits=[tok_c2], dma=True)
            out_free[opar] = tok_od
            if tsub == 1:
                srd_free = tok_c1
        xnB_free[par] = ("t", plan.cnt["t"])


_NC_CACHE = {}


def _get_program(L=4096):
    if L not in _NC_CACHE:
        _NC_CACHE[L] = build_program(L=L)
    return _NC_CACHE[L]


def kernel(L0_attn_x, L0_attn_Wq, L0_attn_Wk, L0_attn_Wv, L0_attn_Wlr,
           L0_attn_Wbeta, L0_attn_Wo, L0_attn_W_in_init, L0_attn_W_out_init,
           L0_ffn_W1, L0_ffn_b1, L0_ffn_W2, L0_ffn_b2):
    x = np.ascontiguousarray(np.asarray(L0_attn_x, dtype=np.float32))
    B, L, _ = x.shape
    base = {
        "wq": np.ascontiguousarray(np.asarray(L0_attn_Wq, np.float32)),
        "wk": np.ascontiguousarray(np.asarray(L0_attn_Wk, np.float32)),
        "wv": np.ascontiguousarray(np.asarray(L0_attn_Wv, np.float32)),
        "wo": np.ascontiguousarray(np.asarray(L0_attn_Wo, np.float32)),
        "wlr": np.ascontiguousarray(np.asarray(L0_attn_Wlr, np.float32)),
        "wbeta": np.ascontiguousarray(np.asarray(L0_attn_Wbeta, np.float32)),
        "winit": np.ascontiguousarray(
            np.asarray(L0_attn_W_in_init, np.float32).reshape(64, 16 * 64)),
        "woutit": np.ascontiguousarray(
            np.asarray(L0_attn_W_out_init, np.float32).reshape(64, 16 * 64)),
        "w1": np.ascontiguousarray(np.asarray(L0_ffn_W1, np.float32)),
        "b1": np.ascontiguousarray(np.asarray(L0_ffn_b1, np.float32)),
        "w2": np.ascontiguousarray(np.asarray(L0_ffn_W2, np.float32)),
        "b2": np.ascontiguousarray(np.asarray(L0_ffn_b2, np.float32)),
    }
    base.update(host_constants())
    nc = _get_program(L=L)
    in_maps = [dict(base, x=x[i]) for i in range(B)]
    res = run_bass_kernel_spmd(nc, in_maps, core_ids=list(range(B)))
    return np.stack([res.results[i]["out"] for i in range(B)], axis=0)
